# version 15
# baseline (speedup 1.0000x reference)
import numpy as np
import concourse.bass as bass
import concourse.mybir as mybir
from concourse.bass_utils import run_bass_kernel_spmd

F32 = mybir.dt.float32
BF16 = mybir.dt.bfloat16
AF = mybir.ActivationFunctionType
ALU = mybir.AluOpType

D = 1024
DC = 8
E = 2048
NS = 16
NG = 48
H_ = 4096
HC = 32
S = 16384
T = 2048
NCORE = 8
SEG = 512
NSEG = S // SEG
TF = 512
NSUB = T // TF
T4 = 4 * SEG
EPS = 1e-5
INV_D = 1.0 / D

DEBUG_OUTS = False
DEBUG_P1 = True
DEBUG_P2 = True
DEBUG_FFN = True


class _Sem:
    __slots__ = ("h", "n", "name", "emitted")

    def __init__(self, nc, name):
        self.h = nc.alloc_semaphore(name=name)
        self.n = 0
        self.name = name
        self.emitted = 0


class _Prog:
    QDEPTH = 6

    def __init__(self, nc):
        self.nc = nc
        self.ops = []
        self.sems = {}
        self.qcount = {}
        self.qlast = {}

    def sem(self, name):
        if name not in self.sems:
            self.sems[name] = _Sem(self.nc, name)
        return self.sems[name]

    def add(self, eng, fn, deps=(), inc=None, dma=False):
        tok = None
        deps = list(d for d in deps if d is not None)
        if inc is not None:
            if dma:
                i = self.qcount.get(inc, 0)
                self.qcount[inc] = i + 1
                name = f"{inc}{i % self.QDEPTH}"
                sm = self.sem(name)
                prev = self.qlast.get(name)
                if prev is not None:
                    deps.append(prev)
                sm.n += 16
                tok = (sm, sm.n, eng, True)
                self.qlast[name] = tok
            else:
                sm = self.sem(inc)
                sm.n += 1
                tok = (sm, sm.n, eng, False)
        self.ops.append((eng, fn, tuple(deps), tok))
        return tok

    def emit(self, block):
        by_eng = {"g": [], "s": [], "v": [], "t": [], "y": []}
        for op in self.ops:
            by_eng[op[0]].append(op)

        def make(eng_key):
            def run(engine):
                waited = {}
                for _, fn, deps, tok in by_eng[eng_key]:
                    for sm, val, peng, pdma in deps:
                        if peng == eng_key and not pdma:
                            continue
                        if waited.get(sm.name, -1) >= val:
                            continue
                        engine.wait_ge(sm.h, val)
                        waited[sm.name] = val
                    ins = fn(engine)
                    if tok is not None:
                        sm, val, _, _ = tok
                        ins.then_inc(sm.h, val - sm.emitted)
                        sm.emitted = val
            return run

        block.gpsimd(make("g"))
        block.scalar(make("s"))
        block.tensor(make("t"))
        block.vector(make("v"))
        block.sync(make("y"))


class _Arena:
    def __init__(self, nc, name, nbytes):
        self.t = nc.alloc_sbuf_tensor(name, [128, nbytes], mybir.dt.uint8)
        self.ap = self.t.ap()
        self.off = 0
        self.nbytes = nbytes
        self.marks = []

    def alloc(self, shape, dtype):
        esz = mybir.dt.size(dtype)
        nfree = int(np.prod(shape[1:]))
        nb = nfree * esz
        off = (self.off + 31) & ~31
        assert off + nb <= self.nbytes, (
            f"arena overflow: need {off + nb}, have {self.nbytes}")
        self.off = off + nb
        v = self.ap[:, off:off + nb].bitcast(dtype)
        if len(shape) > 2:
            names = "abcd"[: len(shape) - 1]
            expr = f"p ({' '.join(names)}) -> p {' '.join(names)}"
            v = v.rearrange(expr, **dict(zip(names, shape[1:])))
        if shape[0] != 128:
            v = v[0:shape[0]]
        return v

    def at(self, off, shape, dtype):
        save = self.off
        self.off = off
        v = self.alloc(shape, dtype)
        self.off = max(save, self.off) if save > self.off else self.off
        self.off = save
        return v

    def mark(self):
        self.marks.append(self.off)

    def pop(self):
        self.off = self.marks.pop()


def build_nc():
    nc = bass.Bass("TRN2", target_bir_lowering=False, detect_race_conditions=False)

    d_xT = nc.dram_tensor("xT", [NSEG, 128, DC, SEG], F32, kind="ExternalInput").ap()
    d_mask = nc.dram_tensor("segfl", [48, NSEG], F32, kind="ExternalInput").ap()
    d_Win = nc.dram_tensor("Wi", [E, D], F32, kind="ExternalInput").ap()
    d_AB = nc.dram_tensor("AB", [E, NG], F32, kind="ExternalInput").ap()
    d_C = nc.dram_tensor("Cs", [E, NS], F32, kind="ExternalInput").ap()
    d_WoT = nc.dram_tensor("WoT", [E, D], F32, kind="ExternalInput").ap()
    d_W12 = nc.dram_tensor("W12T", [HC, 128, DC, 256], F32, kind="ExternalInput").ap()
    d_H1 = nc.dram_tensor("H1T", [16, 128, HC, 256], F32, kind="ExternalInput").ap()
    d_H2 = nc.dram_tensor("H2T", [16, 128, HC, 256], F32, kind="ExternalInput").ap()
    d_W3 = nc.dram_tensor("W3T", [4, 128, HC, 256], F32, kind="ExternalInput").ap()
    d_sv = nc.dram_tensor("svecs", [128, 144], F32, kind="ExternalInput").ap()
    d_bv = nc.dram_tensor("bvecs", [128, 24], F32, kind="ExternalInput").ap()
    d_bo = nc.dram_tensor("bout", [1, D], F32, kind="ExternalInput").ap()

    d_y = nc.dram_tensor("y", [D, T], F32, kind="ExternalOutput").ap()
    dbg = {}
    if DEBUG_OUTS:
        dbg["au"] = nc.dram_tensor("dbg_au", [32, S], F32, kind="ExternalOutput").ap()
        dbg["st"] = nc.dram_tensor("dbg_st", [NS, T], F32, kind="ExternalOutput").ap()
        dbg["ust"] = nc.dram_tensor("dbg_ust", [NS, S], F32, kind="ExternalOutput").ap()
        dbg["xnew"] = nc.dram_tensor("dbg_xnew", [D, T], F32, kind="ExternalOutput").ap()
        dbg["x2n"] = nc.dram_tensor("dbg_x2n", [128, 8 * 1024], BF16, kind="ExternalOutput").ap()
        dbg["t1"] = nc.dram_tensor("dbg_t1", [128, HC * 1024], BF16, kind="ExternalOutput").ap()
        dbg["t2"] = nc.dram_tensor("dbg_t2", [128, HC * 1024], BF16, kind="ExternalOutput").ap()
        dbg["t3"] = nc.dram_tensor("dbg_t3", [128, HC * 1024], BF16, kind="ExternalOutput").ap()


    prog = _Prog(nc)
    _vb = {"h": nc.alloc_semaphore(name="vbar"), "n": 0}

    def vb(e, ins):
        _vb["n"] += 1
        ins.then_inc(_vb["h"], 1)
        e.wait_ge(_vb["h"], _vb["n"])

    _sb = {"h": nc.alloc_semaphore(name="sbar"), "n": 0}

    def sb(e, ins):
        _sb["n"] += 1
        ins.then_inc(_sb["h"], 1)
        e.wait_ge(_sb["h"], _sb["n"])

    P = _Arena(nc, "pers", 45 * 1024)
    hcat = P.alloc([128, 8, 49], BF16)
    g_sb = P.alloc([128, 8, 48], BF16)
    wcT = P.alloc([17, 1024], BF16)
    svec = P.alloc([128, 144], F32)
    bvec = P.alloc([128, 24], BF16)
    onescol_f = P.alloc([128, 1], F32)
    onescol_b = P.alloc([128, 1], BF16)
    onesrow_f = P.alloc([128, 128], F32)
    c_sb = P.alloc([48, 2], F32)
    c_col = c_sb[:, 0:1]
    nhbar = c_sb[:, 1:2]
    stb16 = P.alloc([17, 2048], BF16)
    ones2048b = P.alloc([1, 2048], BF16)
    rowsT = P.alloc([128, 2048], F32)
    rows4 = P.alloc([1, 2048], F32)
    rpk = P.alloc([128, 12, 4], F32)
    rpkf = P.alloc([128, 8, 16], F32)
    tmp1 = P.alloc([128, 512], F32)
    tmp2 = P.alloc([128, 512], F32)
    silu_st = [P.alloc([128, 512], F32) for _ in range(4)]

    s1c = svec[:, 0:32]
    s2c = svec[:, 32:64]
    s3c = svec[:, 64:96]
    s4c = svec[:, 96:128]
    s5c = svec[:, 128:136]
    nwc = svec[:, 136:144]
    binc = bvec[:, 0:16]
    nbc = bvec[:, 16:24]

    _rowbase = (0, 32, 64)

    def rowT(i):
        if i == 3:
            return rows4
        b = _rowbase[i]
        return rowsT[b:b + 1, :]

    A0 = _Arena(nc, "a0", 66 * 1024)
    A1 = _Arena(nc, "a1", 96 * 1024)

    pb = [nc.alloc_psum_tensor(f"pb{i}", [128, 512], F32).ap() for i in range(8)]

    A0.mark()
    wi_sb = A0.alloc([128, 16, 128 * 8], BF16)
    wo_sb = A0.alloc([128, 16, 1024], BF16)
    ab_sb = A0.alloc([128, 16, 48], BF16)
    cs_sb = A0.alloc([128, 16, 16], BF16)

    tok_sv = prog.add("y", lambda e: e.dma_start(out=svec, in_=d_sv), inc="qy", dma=True)
    tok_bv = prog.add("g", lambda e: e.dma_start(out=bvec, in_=d_bv), inc="qg", dma=True)
    tok_wi = prog.add("g", lambda e: e.dma_start(
        out=wi_sb, in_=d_Win.rearrange("(k p) d -> p k d", p=128)), inc="qg", dma=True)
    tok_wo = prog.add("g", lambda e: e.dma_start(
        out=wo_sb, in_=d_WoT.rearrange("(k p) d -> p k d", p=128)), inc="qg", dma=True)
    tok_ab = prog.add("g", lambda e: e.dma_start(
        out=ab_sb, in_=d_AB.rearrange("(k p) n -> p k n", p=128)), inc="qg", dma=True)
    tok_cs = prog.add("g", lambda e: e.dma_start(
        out=cs_sb, in_=d_C.rearrange("(k p) n -> p k n", p=128)), inc="qg", dma=True)
    tok_bo = prog.add("g", lambda e: e.dma_start(out=wcT[16:17, :], in_=d_bo),
                      inc="qg", dma=True)

    tok_of = prog.add("v", lambda e: e.memset(onescol_f, 1.0), inc="vec")
    tok_ob = prog.add("v", lambda e: e.memset(onescol_b, 1.0), inc="vec")
    tok_or = prog.add("v", lambda e: e.memset(onesrow_f[:, :], 1.0), inc="vec")
    tok_hc1 = prog.add("v", lambda e: e.memset(hcat[:, :, 48:49], 1.0), inc="vec")
    tok_o2k = prog.add("v", lambda e: e.memset(ones2048b, 1.0), inc="vec")
    tok_st1 = prog.add("y", lambda e: e.dma_start(out=stb16[16:17, :], in_=ones2048b),
                       deps=[tok_o2k], inc="qy", dma=True)

    g_mm = []
    tok_h = []
    for dc in range(DC):
        bank = pb[dc % 2]

        def mm_g(e, dc=dc, bank=bank):
            last = None
            for k in range(16):
                last = e.matmul(bank[0:128, 0:48],
                                wi_sb[:, k, 128 * dc:128 * (dc + 1)],
                                ab_sb[:, k, :], start=(k == 0), stop=(k == 15))
            return last
        deps = [tok_wi, tok_ab]
        if dc >= 2:
            deps += list(tok_h[dc - 2])
        tmm = prog.add("t", mm_g, deps=deps, inc="tmm")
        g_mm.append(tmm)
        tg = prog.add("v", lambda e, dc=dc, bank=bank: e.tensor_copy(
            g_sb[:, dc, :], bank[0:128, 0:48]), deps=[tmm], inc="vec")
        th = prog.add("v", lambda e, dc=dc, bank=bank: e.tensor_scalar(
            hcat[:, dc, 0:48], bank[0:128, 0:48], nwc[:, dc:dc + 1], None, ALU.mult),
            deps=[tmm, tok_sv], inc="vec")
        tok_h.append((tg, th))

    def mm_hbar(e):
        last = None
        for dc in range(DC):
            last = e.matmul(pb[2][0:48, 0:1], hcat[:, dc, 0:48], onescol_b,
                            start=(dc == 0), stop=(dc == DC - 1))
        return last
    t_hbar_mm = prog.add("t", mm_hbar,
                         deps=[tok_ob] + [t for pr in tok_h for t in pr], inc="tmm")

    def mm_c(e):
        last = None
        for k in range(16):
            last = e.matmul(pb[3][0:48, 0:1], ab_sb[:, k, :], binc[:, k:k + 1],
                            start=(k == 0), stop=False)
        for dc in range(DC):
            last = e.matmul(pb[3][0:48, 0:1], g_sb[:, dc, :], nbc[:, dc:dc + 1],
                            start=False, stop=(dc == DC - 1))
        return last
    t_c_mm = prog.add("t", mm_c, deps=[tok_ab, tok_bv] + [pr[0] for pr in tok_h],
                      inc="tmm")

    t_nhbar = prog.add("v", lambda e: e.tensor_scalar(
        nhbar, pb[2][0:48, 0:1], -1.0, None, ALU.mult), deps=[t_hbar_mm], inc="vec")
    t_cc = prog.add("v", lambda e: e.tensor_copy(c_col, pb[3][0:48, 0:1]),
                    deps=[t_c_mm], inc="vec")

    t_wc_mm = []
    t_wc = []
    for half in range(2):
        def mm_wc(e, half=half):
            last = None
            for k in range(16):
                last = e.matmul(pb[4 + half][0:16, :], cs_sb[:, k, :],
                                wo_sb[:, k, 512 * half:512 * (half + 1)],
                                start=(k == 0), stop=(k == 15))
            return last
        tmm = prog.add("t", mm_wc, deps=[tok_cs, tok_wo], inc="tmm")
        t_wc_mm.append(tmm)
        t_wc.append(prog.add("v", lambda e, half=half: e.tensor_copy(
            wcT[0:16, 512 * half:512 * (half + 1)], pb[4 + half][0:16, :]),
            deps=[tmm], inc="vec"))

    A0.pop()

    RING = 4
    XRING = 4
    NB = NSEG // 4
    A1.mark()
    _off = {}
    _off["xb0"] = A1.off
    xb = [A1.alloc([128, 8, SEG], BF16) for _ in range(XRING)]
    _off["sq0"] = A1.off
    sq = [A1.alloc([128, 8, SEG], BF16) for _ in range(2)]
    _off["vw"] = A1.off
    vwm = [A1.alloc([49, SEG], F32) for _ in range(8)]
    _off["m2b"] = A1.off
    m2b = [A1.alloc([49, SEG], F32) for _ in range(2)]
    au = [A1.alloc([48, SEG], F32) for _ in range(RING)]
    ust = [A1.alloc([16, SEG], F32) for _ in range(2)]
    _off["srow"] = A1.off
    srow_rg = A1.alloc([128, 2 * T4], F32)
    srow = [srow_rg[0:1, :], srow_rg[32:33, :]]
    _off["end"] = A1.off
    rpk2 = [P.alloc([128, 12, 16], F32) for _ in range(2)]
    segfl = P.alloc([48, NSEG], F32)

    fold_done = g_mm + [t_c_mm] + t_wc_mm

    tok_fl = prog.add("y", lambda e: e.dma_start(out=segfl, in_=d_mask),
                      inc="qy", dma=True)
    xb_g = [None] * XRING
    vwm_g = [None] * 8
    au_g = [None] * RING
    ust_g = [None, None]
    sq_g = [None, None]
    pk_g = [None] * 4
    pm_g = [None, None]
    m2b_g = [None, None]
    rpk2_g = [[], []]
    srow_g = [None, None]
    prev_scan = None
    my_state_tok = []
    last_bc_pair = (t_wc[0], t_wc[1])

    def _produce_seg(b, i):
        bp = b % 2
        if True:
            seg = 4 * b + i
            r8 = seg % 8
            rx = seg % XRING
            pk = pb[seg % 4]

            t_xb = prog.add("g", lambda e, rx=rx, seg=seg: e.dma_start(
                out=xb[rx], in_=d_xT[seg]), deps=[xb_g[rx]], inc="qg", dma=True)

            t_sq = []
            for dc in range(DC):
                t_sq.append(prog.add("s", lambda e, rx=rx, dc=dc, seg=seg: e.activation(
                    sq[seg % 2][:, dc, :], xb[rx][:, dc, :], AF.Square),
                    deps=[t_xb, sq_g[seg % 2]], inc="act"))

            pm = pb[6 + seg % 2]

            def mm_proj(e, rx=rx, seg=seg, pk=pk, pm=pm):
                last = None
                for k in range(DC):
                    e.matmul(pk[0:49, :], hcat[:, k, :], xb[rx][:, k, :],
                             start=(k == 0), stop=(k == DC - 1))
                    last = e.matmul(pm[0:49, :], hcat[:, k, :], sq[seg % 2][:, k, :],
                                    start=(k == 0), stop=(k == DC - 1))
                return last
            depm = [t_xb, tok_hc1] + t_sq + [pk_g[seg % 4], pm_g[seg % 2]]
            if seg == 0:
                depm += [t for pr in tok_h for t in pr] + [t_nhbar, t_cc]
            t_mm = prog.add("t", mm_proj, deps=depm, inc="tmm")
            sq_g[seg % 2] = t_mm
            xb_g[rx] = t_mm

            t_ev = prog.add("s", lambda e, r8=r8, pk=pk: e.copy(vwm[r8], pk[0:49, :]),
                            deps=[t_mm, vwm_g[r8]], inc="act")
            pk_g[seg % 4] = t_ev
            t_ev2 = prog.add("s", lambda e, seg=seg, pm=pm: e.copy(
                m2b[seg % 2], pm[0:49, :]),
                deps=[t_mm, m2b_g[seg % 2]], inc="act")
            pm_g[seg % 2] = t_ev2

            t_p1 = prog.add("y", lambda e, r8=r8, bp=bp, i=i: e.dma_start(
                out=rpk2[bp][:, 0, 4 * i:4 * i + 4], in_=vwm[r8][48:49, :]),
                deps=[t_ev] + rpk2_g[bp], inc="qy", dma=True)
            t_p2 = prog.add("y", lambda e, seg=seg, bp=bp, i=i: e.dma_start(
                out=rpk2[bp][:, 1, 4 * i:4 * i + 4], in_=m2b[seg % 2][48:49, :]),
                deps=[t_ev2] + rpk2_g[bp], inc="qy", dma=True)
            m2b_g[seg % 2] = t_p2
            return (seg, r8, t_ev, t_p1, t_p2)

    def _consume_rm(b, seg_info):
        nonlocal my_rows_tok
        bp = b % 2
        r2 = rpk2[bp]
        packs = [t for si in seg_info for t in si[3:5]]

        def rm1(e, r2=r2):
            vb(e, e.tensor_scalar(r2[:, 2, :], r2[:, 0, :], INV_D, None, ALU.mult))
            vb(e, e.tensor_scalar(r2[:, 2, :], r2[:, 0, :], INV_D, None, ALU.mult))
            e.tensor_scalar(r2[:, 3, :], r2[:, 1, :], INV_D, None, ALU.mult)
            vb(e, e.tensor_scalar(r2[:, 3, :], r2[:, 1, :], INV_D, None, ALU.mult))
            vb(e, e.tensor_tensor(r2[:, 4, :], r2[:, 2, :], r2[:, 2, :], ALU.mult))
            vb(e, e.tensor_tensor(r2[:, 4, :], r2[:, 3, :], r2[:, 4, :], ALU.subtract))
            vb(e, e.tensor_scalar(r2[:, 5, :], r2[:, 4, :], EPS, None, ALU.add))
            return e.engine_nop()
        t_rm1 = prog.add("v", rm1, deps=packs, inc="vec")
        def s1f(e, r2=r2):
            sb(e, e.activation(r2[:, 5, :], r2[:, 5, :], AF.Sqrt))
            return e.nop()
        t_s1 = prog.add("s", s1f, deps=[t_rm1], inc="act")
        def r1f(e, r2=r2):
            vb(e, e.reciprocal(r2[:, 6, :], r2[:, 5, :]))
            return e.engine_nop()
        t_r1 = prog.add("v", r1f, deps=[t_s1], inc="vec")

        def rm2(e, r2=r2):
            vb(e, e.tensor_tensor(r2[:, 7, :], r2[:, 4, :], r2[:, 6, :], ALU.mult))
            vb(e, e.tensor_tensor(r2[:, 7, :], r2[:, 7, :], r2[:, 6, :], ALU.mult))
            vb(e, e.tensor_scalar(r2[:, 7, :], r2[:, 7, :], EPS, None, ALU.add))
            return e.engine_nop()
        t_rm2 = prog.add("v", rm2, deps=[t_r1], inc="vec")
        def s2f(e, r2=r2):
            sb(e, e.activation(r2[:, 7, :], r2[:, 7, :], AF.Sqrt))
            return e.nop()
        t_s2 = prog.add("s", s2f, deps=[t_rm2], inc="act")
        def r2f(e, r2=r2):
            vb(e, e.reciprocal(r2[:, 8, :], r2[:, 7, :]))
            return e.engine_nop()
        t_r2 = prog.add("v", r2f, deps=[t_s2], inc="vec")

        def rm3(e, r2=r2):
            vb(e, e.tensor_tensor(r2[:, 9, :], r2[:, 6, :], r2[:, 8, :], ALU.mult))
            vb(e, e.tensor_tensor(r2[:, 10, :], r2[:, 9, :], r2[:, 2, :], ALU.mult))
            return e.engine_nop()
        t_rm3 = prog.add("v", rm3, deps=[t_r2], inc="vec")

        un_toks = []
        for i in range(4):
            un_toks.append(prog.add("y", lambda e, r2=r2, bp=bp, i=i: e.dma_start(
                out=srow[bp][:, 512 * i:512 * (i + 1)], in_=r2[:, 9, 4 * i:4 * i + 4]),
                deps=[t_rm3, srow_g[bp]], inc="qy", dma=True))
            un_toks.append(prog.add("y", lambda e, r2=r2, bp=bp, i=i: e.dma_start(
                out=srow[bp][:, T4 + 512 * i:T4 + 512 * (i + 1)],
                in_=r2[:, 10, 4 * i:4 * i + 4]),
                deps=[t_rm3, srow_g[bp]], inc="qy", dma=True))
        rpk2_g[bp] = list(un_toks)

        if b == NB - 1:
            def rm4(e, r2=r2):
                e.tensor_scalar(r2[:, 11, :], r2[:, 6, :], 1.0, None, ALU.add)
                vb(e, e.tensor_tensor(r2[:, 3, :], r2[:, 2, :], r2[:, 6, :], ALU.mult))
                return e.engine_nop()
            t_rm4 = prog.add("v", rm4, deps=[t_rm3], inc="vec")
            my_rows_tok = []
            for i in range(4):
                my_rows_tok.append(prog.add("y", lambda e, r2=r2, i=i: e.dma_start(
                    out=rowT(0)[:, 512 * i:512 * (i + 1)],
                    in_=r2[:, 11, 4 * i:4 * i + 4]), deps=[t_rm4], inc="qy", dma=True))
                my_rows_tok.append(prog.add("y", lambda e, r2=r2, i=i: e.dma_start(
                    out=rowT(1)[:, 512 * i:512 * (i + 1)],
                    in_=r2[:, 3, 4 * i:4 * i + 4]), deps=[t_rm4], inc="qy", dma=True))
            rpk2_g[bp] += my_rows_tok

        return un_toks

    def _consume_gseg(b, seg_info, un_toks, i):
        nonlocal prev_scan, last_bc_pair
        bp = b % 2
        if True:
            seg, r8, t_ev, _, _ = seg_info[i]
            r = seg % RING

            def mm_bc(e, bp=bp, i=i):
                ob_ = onesrow_f[32 * bp:32 * bp + 1, 0:48]
                e.matmul(pb[4][0:48, :], ob_,
                         srow[bp][:, 512 * i:512 * (i + 1)], start=True, stop=True)
                return e.matmul(pb[5][0:48, :], ob_,
                                srow[bp][:, T4 + 512 * i:T4 + 512 * (i + 1)],
                                start=True, stop=True)
            t_bc = prog.add("t", mm_bc,
                            deps=[un_toks[2 * i], un_toks[2 * i + 1], tok_or] +
                                 list(last_bc_pair), inc="tmm")
            if i == 3:
                srow_g[bp] = t_bc

            dep_au = [t_bc, t_ev, t_nhbar, t_cc, au_g[r]]

            def ga_a(e, r=r, r8=r8):
                e.tensor_tensor(au[r][0:16, :], vwm[r8][0:16, :], pb[4][0:16, :],
                                ALU.mult)
                return e.scalar_tensor_tensor(
                    au[r][0:16, :], pb[5][0:16, :], nhbar[0:16],
                    au[r][0:16, :], ALU.mult, ALU.add)
            t_a = prog.add("v", ga_a, deps=dep_au, inc="vec")

            def ga_u(e, r=r, r8=r8):
                e.tensor_tensor(au[r][32:48, :], vwm[r8][32:48, :], pb[4][32:48, :],
                                ALU.mult)
                return e.scalar_tensor_tensor(
                    au[r][32:48, :], pb[5][32:48, :], nhbar[32:48],
                    au[r][32:48, :], ALU.mult, ALU.add)
            t_uu = prog.add("v", ga_u, deps=dep_au, inc="vec")
            vwm_g[r8] = t_uu
            t_um = prog.add("v", lambda e, r=r, seg=seg: e.tensor_scalar(
                au[r][32:48, :], au[r][32:48, :], c_col[32:48],
                segfl[32:48, seg:seg + 1], ALU.add, ALU.mult),
                deps=[t_uu, tok_fl], inc="vec")
            t_sg = prog.add("s", lambda e, r=r: e.activation(
                au[r][0:16, :], au[r][0:16, :], AF.Sigmoid,
                bias=c_col[0:16], scale=1.0), deps=[t_a], inc="act")
            last_bc_pair = (t_a, t_um)

            t_sh = prog.add("y", lambda e, r=r, u2=seg % 2: e.dma_start(
                out=ust[u2], in_=au[r][32:48, :]),
                deps=[t_um, ust_g[seg % 2], prev_scan], inc="qy", dma=True)

            u2 = seg % 2

            def scn(e, r=r, u2=u2, seg=seg):
                init = 0.0 if seg == 0 else ust[1 - u2][:, SEG - 1:SEG]
                return e.tensor_tensor_scan(ust[u2], au[r][0:16, :], ust[u2],
                                            init, ALU.mult, ALU.add)
            t_scan = prog.add("v", scn, deps=[t_sh, t_sg, prev_scan], inc="vec")
            prev_scan = t_scan
            au_g[r] = t_scan
            ust_g[seg % 2] = t_scan

            if seg >= NSEG - 4:
                my0 = (seg - (NSEG - 4)) * SEG
                t_cst = prog.add("v", lambda e, u2=seg % 2, my0=my0: e.tensor_copy(
                    stb16[0:16, my0:my0 + SEG], ust[u2]), deps=[t_scan], inc="vec")
                my_state_tok.append(t_cst)
                ust_g[seg % 2] = t_cst
                if DEBUG_OUTS and DEBUG_P1:
                    prog.add("y", lambda e, u2=seg % 2, my0=my0: e.dma_start(
                        out=dbg["st"][:, my0:my0 + SEG], in_=ust[u2]),
                        deps=[t_cst], inc="qy", dma=True)

            if DEBUG_OUTS and DEBUG_P1:
                c0 = seg * SEG
                d1 = prog.add("y", lambda e, r=r, c0=c0: e.dma_start(
                    out=dbg["au"][0:16, c0:c0 + SEG], in_=au[r][0:16, :]),
                    deps=[t_sg, t_um], inc="qy", dma=True)
                d2 = prog.add("y", lambda e, r=r, c0=c0: e.dma_start(
                    out=dbg["au"][16:32, c0:c0 + SEG], in_=au[r][32:48, :]),
                    deps=[t_sg, t_um], inc="qy", dma=True)
                au_g[r] = d2
                prog.add("y", lambda e, u2=seg % 2, c0=c0: e.dma_start(
                    out=dbg["ust"][:, c0:c0 + SEG], in_=ust[u2]),
                    deps=[t_scan], inc="qy", dma=True)
                ust_g[seg % 2] = t_scan

    my_rows_tok = []

    d_xnew = nc.dram_tensor("xnew_sc", [4, 128, 8, 512], F32, kind="Internal").ap()
    d_t23 = nc.dram_tensor("t23_sc", [HC, 128, 1024], BF16, kind="Internal").ap()

    xnt = [A1.at(_off["xb0"] + 8 * 1024, [128, 8, 512], F32),
           A1.at(_off["xb0"] + 24 * 1024, [128, 8, 512], F32)]
    sqf = sq[1]
    m12 = A1.at(_off["xb0"], [33, T], F32)

    pair_guard = [None, None, None, None]
    bcx_guard = None
    xnt_g = [None, None]
    spill_toks = []
    stat_toks = []
    prev_e = [None, None]
    sqf_g = None

    def _p2_tile(tt):
        nonlocal bcx_guard, sqf_g, prev_e
        xv = xnt[tt % 2]
        cols = slice(512 * tt, 512 * (tt + 1))
        dep_x = fold_done + [xnt_g[tt % 2]]
        if tt == 0:
            dep_x += [xb_g[1], xb_g[2]]
        elif tt == 1:
            dep_x += [xb_g[3], sq_g[0]]
        if sqf_g is None:
            sqf_g2 = sq_g[1]
        else:
            sqf_g2 = None
        t_x = prog.add("y", lambda e, xv=xv, tt=tt: e.dma_start(
            out=xv, in_=d_xT[NSEG - 4 + tt]),
            deps=dep_x, inc="qy", dma=True)

        def mm_bcx(e, tt=tt):
            e.matmul(pb[6][0:128, :], onesrow_f[0:1, :],
                     rowT(0)[:, 512 * tt:512 * (tt + 1)], start=True, stop=True)
            return e.matmul(pb[7][0:128, :], onesrow_f[32:33, :].bitcast(F32),
                            rowT(1)[:, 512 * tt:512 * (tt + 1)], start=True, stop=True)
        t_bcx = prog.add("t", mm_bcx,
                         deps=[tok_or, my_state_tok[tt], bcx_guard,
                               prev_e[0], prev_e[1], pm_g[0], pm_g[1]] + my_rows_tok,
                         inc="tmm")
        ep_toks = []
        for dc in range(DC):
            bank = pb[dc % 4]

            def mm_wc2(e, dc=dc, cols=cols, bank=bank):
                return e.matmul(bank[0:128, :], wcT[:, 128 * dc:128 * (dc + 1)],
                                stb16[:, cols], start=True, stop=True)
            depm = [t_wc[0], t_wc[1], tok_bo, tok_st1, pair_guard[dc % 4],
                    pk_g[dc % 4], my_state_tok[tt]]
            t_mmo = prog.add("t", mm_wc2, deps=depm, inc="tmm")

            def ep(e, dc=dc, xv=xv, bank=bank):
                e.tensor_tensor(tmp1, xv[:, dc, :], pb[6][0:128, :], ALU.mult)
                e.scalar_tensor_tensor(tmp2, pb[7][0:128, :], -1.0, tmp1,
                                       ALU.mult, ALU.add)
                return e.tensor_tensor(xv[:, dc, :], tmp2, bank[0:128, :], ALU.add)
            t_ep = prog.add("v", ep, deps=[t_mmo, t_bcx, t_x], inc="vec")
            pair_guard[dc % 4] = t_ep
            ep_toks.append(t_ep)
        bcx_guard = ep_toks[-1]

        t_sqs = []
        for dc in range(DC):
            t_sqs.append(prog.add("s", lambda e, dc=dc, xv=xv: e.activation(
                sqf[:, dc, :], xv[:, dc, :], AF.Square),
                deps=[ep_toks[dc], prev_e[1], sqf_g2], inc="act"))

        def mm_m1(e, xv=xv):
            last = None
            for k in range(DC):
                last = e.matmul(pb[6][0:1, :], onescol_f, xv[:, k, :],
                                start=(k == 0), stop=(k == DC - 1))
            return last
        t_m1 = prog.add("t", mm_m1,
                        deps=ep_toks + [tok_of, prev_e[0]],
                        inc="tmm")

        def mm_m2(e):
            last = None
            for k in range(DC):
                last = e.matmul(pb[7][0:1, :], onescol_b, sqf[:, k, :],
                                start=(k == 0), stop=(k == DC - 1))
            return last
        t_m2 = prog.add("t", mm_m2, deps=t_sqs + [tok_ob], inc="tmm")
        e1 = prog.add("s", lambda e, cols=cols: e.copy(m12[0:1, cols], pb[6][0:1, :]),
                      deps=[t_m1], inc="act")
        e2 = prog.add("s", lambda e, cols=cols: e.copy(m12[32:33, cols], pb[7][0:1, :]),
                      deps=[t_m2], inc="act")
        prev_e = [e1, e2]
        stat_toks.extend([e1, e2])

        t_sp = prog.add("y", lambda e, xv=xv, tt=tt: e.dma_start(
            out=d_xnew[tt], in_=xv), deps=ep_toks, inc="qy", dma=True)
        spill_toks.append(t_sp)
        xnt_g[tt % 2] = t_sp
        if DEBUG_OUTS and DEBUG_P2:
            for dc in range(DC):
                prog.add("y", lambda e, xv=xv, dc=dc, cols=cols: e.dma_start(
                    out=dbg["xnew"][128 * dc:128 * (dc + 1), cols],
                    in_=xv[:, dc, :]), deps=ep_toks, inc="qy", dma=True)

    def _phase1_driver():
        _si = {}
        _un = {}
        for b in range(NB):
            _si[b] = [_produce_seg(b, 0)]
            if b >= 1:
                _un[b - 1] = _consume_rm(b - 1, _si[b - 1])
            _si[b].append(_produce_seg(b, 1))
            if b >= 1:
                _consume_gseg(b - 1, _si[b - 1], _un[b - 1], 0)
            _si[b].append(_produce_seg(b, 2))
            if b >= 1:
                _consume_gseg(b - 1, _si[b - 1], _un[b - 1], 1)
            _si[b].append(_produce_seg(b, 3))
            if b >= 1:
                _consume_gseg(b - 1, _si[b - 1], _un[b - 1], 2)
                _consume_gseg(b - 1, _si[b - 1], _un[b - 1], 3)
                del _si[b - 1], _un[b - 1]
        _un[NB - 1] = _consume_rm(NB - 1, _si[NB - 1])
        for _i in range(4):
            _consume_gseg(NB - 1, _si[NB - 1], _un[NB - 1], _i)
            _p2_tile(_i)


    _phase1_driver()

    t_pk1 = prog.add("y", lambda e: e.dma_start(out=rpkf[:, 0, :], in_=m12[0:1, :]),
                     deps=stat_toks, inc="qy", dma=True)
    t_pk2 = prog.add("y", lambda e: e.dma_start(out=rpkf[:, 1, :], in_=m12[32:33, :]),
                     deps=stat_toks, inc="qy", dma=True)

    def frm(e):
        rr = rpkf
        vb(e, e.tensor_scalar(rr[:, 2, :], rr[:, 0, :], INV_D, None, ALU.mult))
        vb(e, e.tensor_scalar(rr[:, 2, :], rr[:, 0, :], INV_D, None, ALU.mult))
        e.tensor_scalar(rr[:, 3, :], rr[:, 1, :], INV_D, None, ALU.mult)
        vb(e, e.tensor_scalar(rr[:, 3, :], rr[:, 1, :], INV_D, None, ALU.mult))
        vb(e, e.tensor_tensor(rr[:, 4, :], rr[:, 2, :], rr[:, 2, :], ALU.mult))
        vb(e, e.tensor_tensor(rr[:, 4, :], rr[:, 3, :], rr[:, 4, :], ALU.subtract))
        vb(e, e.tensor_scalar(rr[:, 4, :], rr[:, 4, :], EPS, None, ALU.add))
        return e.engine_nop()
    t_f1 = prog.add("v", frm, deps=[t_pk1, t_pk2], inc="vec")
    def f2f(e):
        sb(e, e.activation(rpkf[:, 4, :], rpkf[:, 4, :], AF.Sqrt))
        return e.nop()
    t_f2 = prog.add("s", f2f, deps=[t_f1], inc="act")
    def f3f(e):
        vb(e, e.reciprocal(rpkf[:, 5, :], rpkf[:, 4, :]))
        return e.engine_nop()
    t_f3 = prog.add("v", f3f, deps=[t_f2], inc="vec")

    def f4(e):
        vb(e, e.tensor_tensor(rpkf[:, 6, :], rpkf[:, 5, :], rpkf[:, 2, :], ALU.mult))
        return e.engine_nop()
    t_f4 = prog.add("v", f4, deps=[t_f3], inc="vec")
    un_ffn = []
    for i in range(4):
        un_ffn.append(prog.add("y", lambda e, i=i: e.dma_start(
            out=rowT(2)[:, 512 * i:512 * (i + 1)], in_=rpkf[:, 5, 4 * i:4 * i + 4]),
            deps=[t_f3], inc="qy", dma=True))
        un_ffn.append(prog.add("y", lambda e, i=i: e.dma_start(
            out=rowT(3)[:, 512 * i:512 * (i + 1)], in_=rpkf[:, 6, 4 * i:4 * i + 4]),
            deps=[t_f4], inc="qy", dma=True))

    t1b = A0.alloc([128, HC, 1024], BF16)
    x2n = A1.at(_off["srow"], [128, 8, 1024], BF16)
    xnf = xnt
    pan12 = [A1.at(_off["xb0"] + 4096, [128, 8, 256], BF16),
             A1.at(_off["sq0"] + 8 * 1024, [128, 8, 256], BF16)]
    panh = [A1.at(_off["vw"], [128, 32, 256], BF16),
            A1.at(_off["m2b"], [128, 32, 256], BF16)]
    t2st = [A1.at(_off["xb0"] + 2048 * i, [128, 2, 512], BF16) for i in range(2)]

    pan12_use = [t_pk2, stat_toks[-1]]
    panh_use = [my_state_tok[-1], my_state_tok[-1]]
    t1_guard = my_state_tok[-1]
    t2st_g = [t_pk1, t_pk2]
    xnf_g = [spill_toks[2], spill_toks[3]]
    x2n_guard = None
    out_toks = []
    mm_prev = [pair_guard[0], pair_guard[1], pair_guard[2], pair_guard[3]]
    sl_guard = [None, None, None, None]
    bcf_guard = [bcx_guard]

    for j in range(2):
        xh_toks = []
        x2n_toks = []
        for h in range(2):
            tile = 2 * j + h
            t_xh = prog.add("y", lambda e, h=h, tile=tile: e.dma_start(
                out=xnf[h], in_=d_xnew[tile]),
                deps=[spill_toks[tile], xnf_g[h]], inc="qy", dma=True)
            xh_toks.append(t_xh)

            def mm_bcf(e, tile=tile):
                e.matmul(pb[4][0:128, :], onesrow_f[64:65, :],
                         rowT(2)[:, 512 * tile:512 * (tile + 1)], start=True, stop=True)
                return e.matmul(pb[5][0:128, :], onesrow_f[0:1, :],
                                rowT(3)[:, 512 * tile:512 * (tile + 1)],
                                start=True, stop=True)
            t_bcf = prog.add("t", mm_bcf, deps=un_ffn + [tok_or] + bcf_guard,
                             inc="tmm")
            for dc in range(DC):
                def mk_x2n(e, dc=dc, h=h):
                    e.scalar_tensor_tensor(tmp1, pb[5][0:128, :], -1.0,
                                           xnf[h][:, dc, :], ALU.mult, ALU.add)
                    return e.tensor_tensor(x2n[:, dc, 512 * h:512 * (h + 1)],
                                           tmp1, pb[4][0:128, :], ALU.mult)
                dep = [t_bcf, t_xh, x2n_guard]
                if x2n_toks:
                    dep.append(x2n_toks[-1])
                x2n_toks.append(prog.add("v", mk_x2n, deps=dep, inc="vec"))
            bcf_guard = [x2n_toks[-1]]
        x2n_ready = x2n_toks[-1]

        for c in range(HC):
            slot = c % 2

            def ld12(e, c=c, slot=slot):
                return e.dma_start(out=pan12[slot], in_=d_W12[c])
            t_ld = prog.add("g", ld12, deps=[pan12_use[slot]], inc="qg", dma=True)
            sl_toks = [None, None]
            for n in range(2):
                ssl = 2 * (c % 2) + n
                def mmU1(e, slot=slot, n=n):
                    last = None
                    for k in range(DC):
                        last = e.matmul(pb[n][0:128, :], pan12[slot][:, k, 0:128],
                                        x2n[:, k, 512 * n:512 * (n + 1)],
                                        start=(k == 0), stop=(k == DC - 1))
                    return last
                tU1 = prog.add("t", mmU1, deps=[t_ld, x2n_ready, mm_prev[n]],
                               inc="tmm")
                t_sl = prog.add("s", lambda e, c=c, n=n, ssl=ssl: e.activation(
                    silu_st[ssl], pb[n][0:128, :], AF.Silu, bias=0.0,
                    scale=s1c[:, c:c + 1]),
                    deps=[tU1, tok_sv, sl_guard[ssl]], inc="act")
                sl_toks[n] = t_sl
                mm_prev[n] = t_sl
            for n in range(2):
                ssl = 2 * (c % 2) + n
                def mmU2(e, slot=slot, n=n):
                    last = None
                    for k in range(DC):
                        last = e.matmul(pb[2 + n][0:128, :],
                                        pan12[slot][:, k, 128:256],
                                        x2n[:, k, 512 * n:512 * (n + 1)],
                                        start=(k == 0), stop=(k == DC - 1))
                    return last
                tU2 = prog.add("t", mmU2, deps=[t_ld, x2n_ready, mm_prev[2 + n]],
                               inc="tmm")
                dep_t1 = [tU2, sl_toks[n]]
                if c == 0 and n == 0:
                    dep_t1.append(t1_guard)
                tT1 = prog.add("v", lambda e, c=c, n=n, ssl=ssl: e.scalar_tensor_tensor(
                    t1b[:, c, 512 * n:512 * (n + 1)], pb[2 + n][0:128, :],
                    s2c[:, c:c + 1], silu_st[ssl], ALU.mult, ALU.mult),
                    deps=dep_t1, inc="vec")
                mm_prev[2 + n] = tT1
                sl_guard[ssl] = tT1
                pan12_use[slot] = tU2
                last_T1 = tT1

        def h_layer(dsrc, scl_col, kind, rd_guard):
            nonlocal mm_prev
            st_toks_l = []
            npan = 16 if kind == "h" else 4
            for p in range(npan):
                slot = p % 2

                def ldp(e, p=p, slot=slot, dsrc=dsrc):
                    return e.dma_start(out=panh[slot], in_=dsrc[p])
                t_ld = prog.add("g", ldp, deps=[panh_use[slot]], inc="qg", dma=True)
                for q in range(2):
                    oc = 2 * p + q
                    for n in range(2):
                        bk = 2 * q + n

                        def mmH(e, slot=slot, q=q, n=n, bk=bk):
                            last = None
                            for k in range(HC):
                                last = e.matmul(
                                    pb[bk][0:128, :],
                                    panh[slot][:, k, 128 * q:128 * (q + 1)],
                                    t1b[:, k, 512 * n:512 * (n + 1)],
                                    start=(k == 0), stop=(k == HC - 1))
                            return last
                        tH = prog.add("t", mmH,
                                      deps=[t_ld, rd_guard, mm_prev[bk]], inc="tmm")
                        panh_use[slot] = tH
                        if kind == "h":
                            tG = prog.add("s", lambda e, oc=oc, n=n, bk=bk:
                                          e.activation(
                                              t2st[oc % 2][:, n, :], pb[bk][0:128, :],
                                              AF.Gelu, bias=0.0,
                                              scale=scl_col[:, oc:oc + 1]),
                                          deps=[tH, t2st_g[oc % 2]], inc="act")
                            mm_prev[bk] = tG
                            if n == 1:
                                t_st = prog.add("y", lambda e, oc=oc: e.dma_start(
                                    out=d_t23[oc], in_=t2st[oc % 2].rearrange(
                                        "p a b -> p (a b)")),
                                    deps=[mm_prev[2 * q], mm_prev[2 * q + 1]],
                                    inc="qy", dma=True)
                                t2st_g[oc % 2] = t_st
                                st_toks_l.append(t_st)
                        else:
                            dc = oc
                            tO = prog.add("v", lambda e, dc=dc, n=n, bk=bk:
                                          e.scalar_tensor_tensor(
                                              xnf[n][:, dc, :], pb[bk][0:128, :],
                                              scl_col[:, dc:dc + 1], xnf[n][:, dc, :],
                                              ALU.mult, ALU.add),
                                          deps=[tH, tok_sv, xh_toks[n]], inc="vec")
                            mm_prev[bk] = tO
                            st_toks_l.append(tO)
            return st_toks_l

        if DEBUG_OUTS and DEBUG_FFN and j == 0:
            prog.add("y", lambda e: e.dma_start(
                out=dbg["x2n"], in_=x2n.rearrange("p a b -> p (a b)")),
                deps=[x2n_ready], inc="qy", dma=True)
            prog.add("y", lambda e: e.dma_start(
                out=dbg["t1"], in_=t1b.rearrange("p a b -> p (a b)")),
                deps=[last_T1], inc="qy", dma=True)
        h1_st = h_layer(d_H1, s3c, "h", last_T1)
        t_rel = prog.add("y", lambda e: e.dma_start(
            out=t1b.rearrange("p a b -> p (a b)"),
            in_=d_t23.transpose([1, 0, 2])),
            deps=h1_st + [panh_use[0], panh_use[1]], inc="qy", dma=True)
        if DEBUG_OUTS and DEBUG_FFN and j == 0:
            prog.add("y", lambda e: e.dma_start(
                out=dbg["t2"], in_=t1b.rearrange("p a b -> p (a b)")),
                deps=[t_rel], inc="qy", dma=True)
        h2_st = h_layer(d_H2, s4c, "h", t_rel)
        t_rel2 = prog.add("y", lambda e: e.dma_start(
            out=t1b.rearrange("p a b -> p (a b)"),
            in_=d_t23.transpose([1, 0, 2])),
            deps=h2_st + [panh_use[0], panh_use[1]], inc="qy", dma=True)
        if DEBUG_OUTS and DEBUG_FFN and j == 0:
            prog.add("y", lambda e: e.dma_start(
                out=dbg["t3"], in_=t1b.rearrange("p a b -> p (a b)")),
                deps=[t_rel2], inc="qy", dma=True)
        w3_out = h_layer(d_W3, s5c, "w3", t_rel2)

        for n in range(2):
            colbase = 1024 * j + 512 * n
            for dc in range(DC):
                t_st = prog.add("y", lambda e, dc=dc, n=n, colbase=colbase: e.dma_start(
                    out=d_y[128 * dc:128 * (dc + 1), colbase:colbase + 512],
                    in_=xnf[n][:, dc, :]), deps=list(w3_out), inc="qy", dma=True)
                out_toks.append(t_st)
            xnf_g[n] = out_toks[-1]
        x2n_guard = w3_out[-1]
        t1_guard = w3_out[-1]

    if DEBUG_OUTS:
        dbg["rows"] = nc.dram_tensor("dbg_rows", [128, 2048 + 192], F32,
                                     kind="ExternalOutput").ap()
        dbg["stb"] = nc.dram_tensor("dbg_stb", [17, 2048], BF16,
                                    kind="ExternalOutput").ap()
        dbg["xsp"] = nc.dram_tensor("dbg_xsp", [4, 128, 8, 512], F32,
                                    kind="ExternalOutput").ap()
        prog.add("y", lambda e: e.dma_start(out=dbg["stb"], in_=stb16),
                 deps=out_toks, inc="qy", dma=True)
        prog.add("y", lambda e: e.dma_start(out=dbg["xsp"], in_=d_xnew),
                 deps=out_toks, inc="qy", dma=True)
        for _rr, _b in ((0, 0), (1, 32), (2, 64)):
            prog.add("y", lambda e, _rr=_rr, _b=_b: e.dma_start(
                out=dbg["rows"][_rr:_rr + 1, 0:2048], in_=rowsT[_b:_b + 1, :]),
                deps=out_toks, inc="qy", dma=True)
        prog.add("y", lambda e: e.dma_start(
            out=dbg["rows"][:, 2048:2048 + 112],
            in_=rpkf[:, 0:7, :].rearrange("p a b -> p (a b)")),
            deps=out_toks, inc="qy", dma=True)
        dbg["rpk2"] = nc.dram_tensor("dbg_rpk2", [128, 2, 11, 16], F32,
                                     kind="ExternalOutput").ap()
        for _pp in range(2):
            prog.add("y", lambda e, _pp=_pp: e.dma_start(
                out=dbg["rpk2"][:, _pp], in_=rpk2[_pp][:, 0:11, :]),
                deps=out_toks, inc="qy", dma=True)
        prog.add("y", lambda e: e.dma_start(
            out=dbg["rows"][0:1, 2048 + 128:2048 + 160].bitcast(F32) if False
            else dbg["rows"][:, 2048 + 160:2048 + 176], in_=rows4),
            deps=out_toks, inc="qy", dma=True)

    prog.add("y", lambda e: e.nop(), deps=out_toks)

    with nc.Block() as block:
        prog.emit(block)
    return nc


_CACHE = {}


def _prep_shared(inputs):
    f = np.float32

    def c(a):
        return np.ascontiguousarray(np.asarray(a, dtype=f))
    x = c(inputs["L0_attn_x"]).reshape(S, D)
    xT = np.ascontiguousarray(x.T)
    Wi = c(inputs["L0_attn_W_in"])
    AB = np.zeros((E, NG), dtype=f)
    AB[:, 0:16] = c(inputs["L0_attn_A"])
    AB[:, 32:48] = c(inputs["L0_attn_B_ssm"])
    Cs = c(inputs["L0_attn_C_ssm"])
    WoT = np.ascontiguousarray(c(inputs["L0_attn_W_out"]).T)
    w1T = c(inputs["L0_ffn_w1_w"]).T
    w2T = c(inputs["L0_ffn_w2_w"]).T
    W12 = np.empty((D, 2 * H_), dtype=f)
    for i in range(HC):
        W12[:, 256 * i:256 * i + 128] = w1T[:, 128 * i:128 * (i + 1)]
        W12[:, 256 * i + 128:256 * i + 256] = w2T[:, 128 * i:128 * (i + 1)]

    def _paneled(wT, npan):
        K, O = wT.shape
        kc = K // 128
        opan = O // npan
        return np.ascontiguousarray(
            wT.reshape(kc, 128, npan, opan).transpose(2, 1, 0, 3))
    sv = np.zeros((128, 144), dtype=f)
    sv[:, 0:32] = c(inputs["L0_ffn_w1_s"]).reshape(32, 128).T
    sv[:, 32:64] = c(inputs["L0_ffn_w2_s"]).reshape(32, 128).T
    sv[:, 64:96] = c(inputs["L0_ffn_h1_s"]).reshape(32, 128).T
    sv[:, 96:128] = c(inputs["L0_ffn_h2_s"]).reshape(32, 128).T
    sv[:, 128:136] = c(inputs["L0_ffn_w3_s"]).reshape(8, 128).T
    sv[:, 136:144] = c(inputs["L0_attn_norm_w"]).reshape(8, 128).T
    bv = np.zeros((128, 24), dtype=f)
    bv[:, 0:16] = c(inputs["L0_attn_b_in"]).reshape(16, 128).T
    bv[:, 16:24] = c(inputs["L0_attn_norm_b"]).reshape(8, 128).T
    shared = {"Wi": Wi, "AB": AB, "Cs": Cs, "WoT": WoT,
              "W12T": _paneled(W12, HC),
              "H1T": _paneled(c(inputs["L0_ffn_h1_w"]).T, 16),
              "H2T": _paneled(c(inputs["L0_ffn_h2_w"]).T, 16),
              "W3T": _paneled(c(inputs["L0_ffn_w3_w"]).T, 4),
              "svecs": sv, "bvecs": bv,
              "bout": c(inputs["L0_attn_b_out"]).reshape(1, D)}
    return xT, shared


def make_in_maps(inputs):
    xT, shared = _prep_shared(inputs)
    in_maps = []
    for cidx in range(NCORE):
        pad = T * (NCORE - 1 - cidx)
        xTc = np.roll(xT, pad, axis=1) if pad else xT
        xt4 = np.ascontiguousarray(
            xTc.reshape(DC, 128, NSEG, SEG).transpose(2, 1, 0, 3))
        fl = np.zeros((48, NSEG), dtype=np.float32)
        fl[:, (pad // SEG):] = 1.0
        m = dict(shared)
        m["xT"] = xt4
        m["segfl"] = fl
        in_maps.append(m)
    return in_maps


def kernel(**inputs):
    if "nc" not in _CACHE:
        _CACHE["nc"] = build_nc()
    nc = _CACHE["nc"]
    in_maps = make_in_maps(inputs)
    res = run_bass_kernel_spmd(nc, in_maps, core_ids=list(range(NCORE)))
    y = np.concatenate([np.asarray(r["y"]) for r in res.results], axis=1)
    return np.ascontiguousarray(y.T).reshape(1, S, D)


# version 16
# speedup vs baseline: 1.0596x; 1.0596x over previous
import numpy as np
import concourse.bass as bass
import concourse.mybir as mybir
from concourse.bass_utils import run_bass_kernel_spmd

F32 = mybir.dt.float32
BF16 = mybir.dt.bfloat16
AF = mybir.ActivationFunctionType
ALU = mybir.AluOpType

D = 1024
DC = 8
E = 2048
NS = 16
NG = 48
H_ = 4096
HC = 32
S = 16384
T = 2048
NCORE = 8
SEG = 512
NSEG = S // SEG
TF = 512
NSUB = T // TF
T4 = 4 * SEG
EPS = 1e-5
INV_D = 1.0 / D

DEBUG_OUTS = False
DEBUG_P1 = True
DEBUG_P2 = True
DEBUG_FFN = True


class _Sem:
    __slots__ = ("h", "n", "name", "emitted")

    def __init__(self, nc, name):
        self.h = nc.alloc_semaphore(name=name)
        self.n = 0
        self.name = name
        self.emitted = 0


class _Prog:
    QDEPTH = 6

    def __init__(self, nc):
        self.nc = nc
        self.ops = []
        self.sems = {}
        self.qcount = {}
        self.qlast = {}

    def sem(self, name):
        if name not in self.sems:
            self.sems[name] = _Sem(self.nc, name)
        return self.sems[name]

    def add(self, eng, fn, deps=(), inc=None, dma=False):
        tok = None
        deps = list(d for d in deps if d is not None)
        if inc is not None:
            if dma:
                i = self.qcount.get(inc, 0)
                self.qcount[inc] = i + 1
                name = f"{inc}{i % self.QDEPTH}"
                sm = self.sem(name)
                prev = self.qlast.get(name)
                if prev is not None:
                    deps.append(prev)
                sm.n += 16
                tok = (sm, sm.n, eng, True)
                self.qlast[name] = tok
            else:
                sm = self.sem(inc)
                sm.n += 1
                tok = (sm, sm.n, eng, False)
        self.ops.append((eng, fn, tuple(deps), tok))
        return tok

    def emit(self, block):
        by_eng = {"g": [], "s": [], "v": [], "t": [], "y": []}
        for op in self.ops:
            by_eng[op[0]].append(op)

        def make(eng_key):
            def run(engine):
                waited = {}
                for _, fn, deps, tok in by_eng[eng_key]:
                    for sm, val, peng, pdma in deps:
                        if peng == eng_key and not pdma:
                            continue
                        if waited.get(sm.name, -1) >= val:
                            continue
                        engine.wait_ge(sm.h, val)
                        waited[sm.name] = val
                    ins = fn(engine)
                    if tok is not None:
                        sm, val, _, _ = tok
                        ins.then_inc(sm.h, val - sm.emitted)
                        sm.emitted = val
            return run

        block.gpsimd(make("g"))
        block.scalar(make("s"))
        block.tensor(make("t"))
        block.vector(make("v"))
        block.sync(make("y"))


class _Arena:
    def __init__(self, nc, name, nbytes):
        self.t = nc.alloc_sbuf_tensor(name, [128, nbytes], mybir.dt.uint8)
        self.ap = self.t.ap()
        self.off = 0
        self.nbytes = nbytes
        self.marks = []

    def alloc(self, shape, dtype):
        esz = mybir.dt.size(dtype)
        nfree = int(np.prod(shape[1:]))
        nb = nfree * esz
        off = (self.off + 31) & ~31
        assert off + nb <= self.nbytes, (
            f"arena overflow: need {off + nb}, have {self.nbytes}")
        self.off = off + nb
        v = self.ap[:, off:off + nb].bitcast(dtype)
        if len(shape) > 2:
            names = "abcd"[: len(shape) - 1]
            expr = f"p ({' '.join(names)}) -> p {' '.join(names)}"
            v = v.rearrange(expr, **dict(zip(names, shape[1:])))
        if shape[0] != 128:
            v = v[0:shape[0]]
        return v

    def at(self, off, shape, dtype):
        save = self.off
        self.off = off
        v = self.alloc(shape, dtype)
        self.off = max(save, self.off) if save > self.off else self.off
        self.off = save
        return v

    def mark(self):
        self.marks.append(self.off)

    def pop(self):
        self.off = self.marks.pop()


def build_nc():
    nc = bass.Bass("TRN2", target_bir_lowering=False, detect_race_conditions=False)

    d_xT = nc.dram_tensor("xT", [NSEG, 128, DC, SEG], F32, kind="ExternalInput").ap()
    d_mask = nc.dram_tensor("segfl", [48, NSEG], F32, kind="ExternalInput").ap()
    d_Win = nc.dram_tensor("Wi", [E, D], F32, kind="ExternalInput").ap()
    d_AB = nc.dram_tensor("AB", [E, NG], F32, kind="ExternalInput").ap()
    d_C = nc.dram_tensor("Cs", [E, NS], F32, kind="ExternalInput").ap()
    d_WoT = nc.dram_tensor("WoT", [E, D], F32, kind="ExternalInput").ap()
    d_W12 = nc.dram_tensor("W12T", [HC, 128, DC, 256], F32, kind="ExternalInput").ap()
    d_H1 = nc.dram_tensor("H1T", [16, 128, HC, 256], F32, kind="ExternalInput").ap()
    d_H2 = nc.dram_tensor("H2T", [16, 128, HC, 256], F32, kind="ExternalInput").ap()
    d_W3 = nc.dram_tensor("W3T", [4, 128, HC, 256], F32, kind="ExternalInput").ap()
    d_sv = nc.dram_tensor("svecs", [128, 144], F32, kind="ExternalInput").ap()
    d_bv = nc.dram_tensor("bvecs", [128, 24], F32, kind="ExternalInput").ap()
    d_bo = nc.dram_tensor("bout", [1, D], F32, kind="ExternalInput").ap()

    d_y = nc.dram_tensor("y", [D, T], F32, kind="ExternalOutput").ap()
    dbg = {}
    if DEBUG_OUTS:
        dbg["au"] = nc.dram_tensor("dbg_au", [32, S], F32, kind="ExternalOutput").ap()
        dbg["st"] = nc.dram_tensor("dbg_st", [NS, T], F32, kind="ExternalOutput").ap()
        dbg["ust"] = nc.dram_tensor("dbg_ust", [NS, S], F32, kind="ExternalOutput").ap()
        dbg["xnew"] = nc.dram_tensor("dbg_xnew", [D, T], F32, kind="ExternalOutput").ap()
        dbg["x2n"] = nc.dram_tensor("dbg_x2n", [128, 8 * 1024], BF16, kind="ExternalOutput").ap()
        dbg["t1"] = nc.dram_tensor("dbg_t1", [128, HC * 1024], BF16, kind="ExternalOutput").ap()
        dbg["t2"] = nc.dram_tensor("dbg_t2", [128, HC * 1024], BF16, kind="ExternalOutput").ap()
        dbg["t3"] = nc.dram_tensor("dbg_t3", [128, HC * 1024], BF16, kind="ExternalOutput").ap()


    prog = _Prog(nc)
    _vb = {"h": nc.alloc_semaphore(name="vbar"), "n": 0}

    def vb(e, ins):
        _vb["n"] += 1
        ins.then_inc(_vb["h"], 1)
        e.wait_ge(_vb["h"], _vb["n"])

    _sb = {"h": nc.alloc_semaphore(name="sbar"), "n": 0}

    def sb(e, ins):
        _sb["n"] += 1
        ins.then_inc(_sb["h"], 1)
        e.wait_ge(_sb["h"], _sb["n"])

    P = _Arena(nc, "pers", 45 * 1024)
    hcat = P.alloc([128, 8, 49], BF16)
    g_sb = P.alloc([128, 8, 48], BF16)
    wcT = P.alloc([17, 1024], BF16)
    svec = P.alloc([128, 144], F32)
    bvec = P.alloc([128, 24], BF16)
    onescol_f = P.alloc([128, 1], F32)
    onescol_b = P.alloc([128, 1], BF16)
    onesrow_f = P.alloc([128, 128], F32)
    c_sb = P.alloc([48, 2], F32)
    c_col = c_sb[:, 0:1]
    nhbar = c_sb[:, 1:2]
    stb16 = P.alloc([17, 2048], BF16)
    ones2048b = P.alloc([1, 2048], BF16)
    rowsT = P.alloc([128, 2048], F32)
    rows4 = P.alloc([1, 2048], F32)
    rpk = P.alloc([128, 12, 4], F32)
    rpkf = P.alloc([128, 8, 16], F32)
    tmp1 = P.alloc([128, 512], F32)
    tmp2 = P.alloc([128, 512], F32)
    silu_st = [P.alloc([128, 512], F32) for _ in range(4)]

    s1c = svec[:, 0:32]
    s2c = svec[:, 32:64]
    s3c = svec[:, 64:96]
    s4c = svec[:, 96:128]
    s5c = svec[:, 128:136]
    nwc = svec[:, 136:144]
    binc = bvec[:, 0:16]
    nbc = bvec[:, 16:24]

    _rowbase = (0, 32, 64)

    def rowT(i):
        if i == 3:
            return rows4
        b = _rowbase[i]
        return rowsT[b:b + 1, :]

    A0 = _Arena(nc, "a0", 66 * 1024)
    A1 = _Arena(nc, "a1", 96 * 1024)

    pb = [nc.alloc_psum_tensor(f"pb{i}", [128, 512], F32).ap() for i in range(8)]

    A0.mark()
    wi_sb = A0.alloc([128, 16, 128 * 8], BF16)
    wo_sb = A0.alloc([128, 16, 1024], BF16)
    ab_sb = A0.alloc([128, 16, 48], BF16)
    cs_sb = A0.alloc([128, 16, 16], BF16)

    tok_sv = prog.add("y", lambda e: e.dma_start(out=svec, in_=d_sv), inc="qy", dma=True)
    tok_bv = prog.add("g", lambda e: e.dma_start(out=bvec, in_=d_bv), inc="qg", dma=True)
    tok_wi = prog.add("g", lambda e: e.dma_start(
        out=wi_sb, in_=d_Win.rearrange("(k p) d -> p k d", p=128)), inc="qg", dma=True)
    tok_wo = prog.add("g", lambda e: e.dma_start(
        out=wo_sb, in_=d_WoT.rearrange("(k p) d -> p k d", p=128)), inc="qg", dma=True)
    tok_ab = prog.add("g", lambda e: e.dma_start(
        out=ab_sb, in_=d_AB.rearrange("(k p) n -> p k n", p=128)), inc="qg", dma=True)
    tok_cs = prog.add("g", lambda e: e.dma_start(
        out=cs_sb, in_=d_C.rearrange("(k p) n -> p k n", p=128)), inc="qg", dma=True)
    tok_bo = prog.add("g", lambda e: e.dma_start(out=wcT[16:17, :], in_=d_bo),
                      inc="qg", dma=True)

    tok_of = prog.add("v", lambda e: e.memset(onescol_f, 1.0), inc="vec")
    tok_ob = prog.add("v", lambda e: e.memset(onescol_b, 1.0), inc="vec")
    tok_or = prog.add("v", lambda e: e.memset(onesrow_f[:, :], 1.0), inc="vec")
    tok_hc1 = prog.add("v", lambda e: e.memset(hcat[:, :, 48:49], 1.0), inc="vec")
    tok_o2k = prog.add("v", lambda e: e.memset(ones2048b, 1.0), inc="vec")
    tok_st1 = prog.add("y", lambda e: e.dma_start(out=stb16[16:17, :], in_=ones2048b),
                       deps=[tok_o2k], inc="qy", dma=True)

    g_mm = []
    tok_h = []
    for dc in range(DC):
        bank = pb[dc % 2]

        def mm_g(e, dc=dc, bank=bank):
            last = None
            for k in range(16):
                last = e.matmul(bank[0:128, 0:48],
                                wi_sb[:, k, 128 * dc:128 * (dc + 1)],
                                ab_sb[:, k, :], start=(k == 0), stop=(k == 15))
            return last
        deps = [tok_wi, tok_ab]
        if dc >= 2:
            deps += list(tok_h[dc - 2])
        tmm = prog.add("t", mm_g, deps=deps, inc="tmm")
        g_mm.append(tmm)
        tg = prog.add("v", lambda e, dc=dc, bank=bank: e.tensor_copy(
            g_sb[:, dc, :], bank[0:128, 0:48]), deps=[tmm], inc="vec")
        th = prog.add("v", lambda e, dc=dc, bank=bank: e.tensor_scalar(
            hcat[:, dc, 0:48], bank[0:128, 0:48], nwc[:, dc:dc + 1], None, ALU.mult),
            deps=[tmm, tok_sv], inc="vec")
        tok_h.append((tg, th))

    def mm_hbar(e):
        last = None
        for dc in range(DC):
            last = e.matmul(pb[2][0:48, 0:1], hcat[:, dc, 0:48], onescol_b,
                            start=(dc == 0), stop=(dc == DC - 1))
        return last
    t_hbar_mm = prog.add("t", mm_hbar,
                         deps=[tok_ob] + [t for pr in tok_h for t in pr], inc="tmm")

    def mm_c(e):
        last = None
        for k in range(16):
            last = e.matmul(pb[3][0:48, 0:1], ab_sb[:, k, :], binc[:, k:k + 1],
                            start=(k == 0), stop=False)
        for dc in range(DC):
            last = e.matmul(pb[3][0:48, 0:1], g_sb[:, dc, :], nbc[:, dc:dc + 1],
                            start=False, stop=(dc == DC - 1))
        return last
    t_c_mm = prog.add("t", mm_c, deps=[tok_ab, tok_bv] + [pr[0] for pr in tok_h],
                      inc="tmm")

    t_nhbar = prog.add("v", lambda e: e.tensor_scalar(
        nhbar, pb[2][0:48, 0:1], -1.0, None, ALU.mult), deps=[t_hbar_mm], inc="vec")
    t_cc = prog.add("v", lambda e: e.tensor_copy(c_col, pb[3][0:48, 0:1]),
                    deps=[t_c_mm], inc="vec")

    t_wc_mm = []
    t_wc = []
    for half in range(2):
        def mm_wc(e, half=half):
            last = None
            for k in range(16):
                last = e.matmul(pb[4 + half][0:16, :], cs_sb[:, k, :],
                                wo_sb[:, k, 512 * half:512 * (half + 1)],
                                start=(k == 0), stop=(k == 15))
            return last
        tmm = prog.add("t", mm_wc, deps=[tok_cs, tok_wo], inc="tmm")
        t_wc_mm.append(tmm)
        t_wc.append(prog.add("v", lambda e, half=half: e.tensor_copy(
            wcT[0:16, 512 * half:512 * (half + 1)], pb[4 + half][0:16, :]),
            deps=[tmm], inc="vec"))

    A0.pop()

    RING = 4
    XRING = 4
    NB = NSEG // 4
    A1.mark()
    _off = {}
    _off["xb0"] = A1.off
    xb = [A1.alloc([128, 8, SEG], BF16) for _ in range(XRING)]
    _off["sq0"] = A1.off
    sq = [A1.alloc([128, 8, SEG], BF16) for _ in range(2)]
    _off["vw"] = A1.off
    vwm = [A1.alloc([49, SEG], F32) for _ in range(8)]
    _off["m2b"] = A1.off
    m2b = [A1.alloc([49, SEG], F32) for _ in range(2)]
    au = [A1.alloc([48, SEG], F32) for _ in range(RING)]
    ust = [A1.alloc([16, SEG], F32) for _ in range(2)]
    _off["srow"] = A1.off
    srow_rg = A1.alloc([128, 2 * T4], F32)
    srow = [srow_rg[0:1, :], srow_rg[32:33, :]]
    _off["end"] = A1.off
    rpk2 = [P.alloc([128, 12, 16], F32) for _ in range(2)]
    segfl = P.alloc([48, NSEG], F32)

    fold_done = g_mm + [t_c_mm] + t_wc_mm

    tok_fl = prog.add("y", lambda e: e.dma_start(out=segfl, in_=d_mask),
                      inc="qy", dma=True)
    xb_g = [None] * XRING
    vwm_g = [None] * 8
    au_g = [None] * RING
    ust_g = [None, None]
    sq_g = [None, None]
    pk_g = [None] * 4
    pm_g = [None, None]
    m2b_g = [None, None]
    rpk2_g = [[], []]
    srow_g = [None, None]
    prev_scan = None
    my_state_tok = []
    last_bc_pair = (t_wc[0], t_wc[1])

    def _produce_seg(b, i):
        bp = b % 2
        if True:
            seg = 4 * b + i
            r8 = seg % 8
            rx = seg % XRING
            pk = pb[seg % 4]

            t_xb = prog.add("g", lambda e, rx=rx, seg=seg: e.dma_start(
                out=xb[rx], in_=d_xT[seg]), deps=[xb_g[rx]], inc="qg", dma=True)

            t_sq = []
            for dc in range(DC):
                t_sq.append(prog.add("s", lambda e, rx=rx, dc=dc, seg=seg: e.activation(
                    sq[seg % 2][:, dc, :], xb[rx][:, dc, :], AF.Square),
                    deps=[t_xb, sq_g[seg % 2]], inc="act"))

            pm = pb[6 + seg % 2]

            def mm_proj(e, rx=rx, seg=seg, pk=pk, pm=pm):
                last = None
                for k in range(DC):
                    e.matmul(pk[0:49, :], hcat[:, k, :], xb[rx][:, k, :],
                             start=(k == 0), stop=(k == DC - 1))
                    last = e.matmul(pm[0:49, :], hcat[:, k, :], sq[seg % 2][:, k, :],
                                    start=(k == 0), stop=(k == DC - 1))
                return last
            depm = [t_xb, tok_hc1] + t_sq + [pk_g[seg % 4], pm_g[seg % 2]]
            if seg == 0:
                depm += [t for pr in tok_h for t in pr] + [t_nhbar, t_cc]
            t_mm = prog.add("t", mm_proj, deps=depm, inc="tmm")
            sq_g[seg % 2] = t_mm
            xb_g[rx] = t_mm

            t_ev = prog.add("s", lambda e, r8=r8, pk=pk: e.copy(vwm[r8], pk[0:49, :]),
                            deps=[t_mm, vwm_g[r8]], inc="act")
            pk_g[seg % 4] = t_ev
            t_ev2 = prog.add("s", lambda e, seg=seg, pm=pm: e.copy(
                m2b[seg % 2], pm[0:49, :]),
                deps=[t_mm, m2b_g[seg % 2]], inc="act")
            pm_g[seg % 2] = t_ev2

            t_p1 = prog.add("y", lambda e, r8=r8, bp=bp, i=i: e.dma_start(
                out=rpk2[bp][:, 0, 4 * i:4 * i + 4], in_=vwm[r8][48:49, :]),
                deps=[t_ev] + rpk2_g[bp], inc="qy", dma=True)
            t_p2 = prog.add("y", lambda e, seg=seg, bp=bp, i=i: e.dma_start(
                out=rpk2[bp][:, 1, 4 * i:4 * i + 4], in_=m2b[seg % 2][48:49, :]),
                deps=[t_ev2] + rpk2_g[bp], inc="qy", dma=True)
            m2b_g[seg % 2] = t_p2
            return (seg, r8, t_ev, t_p1, t_p2)

    def _consume_rm(b, seg_info):
        nonlocal my_rows_tok
        bp = b % 2
        r2 = rpk2[bp]
        packs = [t for si in seg_info for t in si[3:5]]

        def rm1(e, r2=r2):
            vb(e, e.tensor_scalar(r2[:, 2, :], r2[:, 0, :], INV_D, None, ALU.mult))
            vb(e, e.tensor_scalar(r2[:, 2, :], r2[:, 0, :], INV_D, None, ALU.mult))
            e.tensor_scalar(r2[:, 3, :], r2[:, 1, :], INV_D, None, ALU.mult)
            vb(e, e.tensor_scalar(r2[:, 3, :], r2[:, 1, :], INV_D, None, ALU.mult))
            vb(e, e.tensor_tensor(r2[:, 4, :], r2[:, 2, :], r2[:, 2, :], ALU.mult))
            vb(e, e.tensor_tensor(r2[:, 4, :], r2[:, 3, :], r2[:, 4, :], ALU.subtract))
            vb(e, e.tensor_scalar(r2[:, 5, :], r2[:, 4, :], EPS, None, ALU.add))
            return e.engine_nop()
        t_rm1 = prog.add("v", rm1, deps=packs, inc="vec")
        def s1f(e, r2=r2):
            sb(e, e.activation(r2[:, 5, :], r2[:, 5, :], AF.Sqrt))
            return e.nop()
        t_s1 = prog.add("s", s1f, deps=[t_rm1], inc="act")
        def r1f(e, r2=r2):
            vb(e, e.reciprocal(r2[:, 6, :], r2[:, 5, :]))
            return e.engine_nop()
        t_r1 = prog.add("v", r1f, deps=[t_s1], inc="vec")

        def rm2(e, r2=r2):
            vb(e, e.tensor_tensor(r2[:, 7, :], r2[:, 4, :], r2[:, 6, :], ALU.mult))
            vb(e, e.tensor_tensor(r2[:, 7, :], r2[:, 7, :], r2[:, 6, :], ALU.mult))
            vb(e, e.tensor_scalar(r2[:, 7, :], r2[:, 7, :], EPS, None, ALU.add))
            return e.engine_nop()
        t_rm2 = prog.add("v", rm2, deps=[t_r1], inc="vec")
        def s2f(e, r2=r2):
            sb(e, e.activation(r2[:, 7, :], r2[:, 7, :], AF.Sqrt))
            return e.nop()
        t_s2 = prog.add("s", s2f, deps=[t_rm2], inc="act")
        def r2f(e, r2=r2):
            vb(e, e.reciprocal(r2[:, 8, :], r2[:, 7, :]))
            return e.engine_nop()
        t_r2 = prog.add("v", r2f, deps=[t_s2], inc="vec")

        def rm3(e, r2=r2):
            vb(e, e.tensor_tensor(r2[:, 9, :], r2[:, 6, :], r2[:, 8, :], ALU.mult))
            vb(e, e.tensor_tensor(r2[:, 10, :], r2[:, 9, :], r2[:, 2, :], ALU.mult))
            return e.engine_nop()
        t_rm3 = prog.add("v", rm3, deps=[t_r2], inc="vec")

        un_toks = []
        for i in range(4):
            un_toks.append(prog.add("y", lambda e, r2=r2, bp=bp, i=i: e.dma_start(
                out=srow[bp][:, 512 * i:512 * (i + 1)], in_=r2[:, 9, 4 * i:4 * i + 4]),
                deps=[t_rm3, srow_g[bp]], inc="qy", dma=True))
            un_toks.append(prog.add("y", lambda e, r2=r2, bp=bp, i=i: e.dma_start(
                out=srow[bp][:, T4 + 512 * i:T4 + 512 * (i + 1)],
                in_=r2[:, 10, 4 * i:4 * i + 4]),
                deps=[t_rm3, srow_g[bp]], inc="qy", dma=True))
        rpk2_g[bp] = list(un_toks)

        if b == NB - 1:
            def rm4(e, r2=r2):
                e.tensor_scalar(r2[:, 11, :], r2[:, 6, :], 1.0, None, ALU.add)
                vb(e, e.tensor_tensor(r2[:, 3, :], r2[:, 2, :], r2[:, 6, :], ALU.mult))
                return e.engine_nop()
            t_rm4 = prog.add("v", rm4, deps=[t_rm3], inc="vec")
            my_rows_tok = []
            for i in range(4):
                my_rows_tok.append(prog.add("y", lambda e, r2=r2, i=i: e.dma_start(
                    out=rowT(0)[:, 512 * i:512 * (i + 1)],
                    in_=r2[:, 11, 4 * i:4 * i + 4]), deps=[t_rm4], inc="qy", dma=True))
                my_rows_tok.append(prog.add("y", lambda e, r2=r2, i=i: e.dma_start(
                    out=rowT(1)[:, 512 * i:512 * (i + 1)],
                    in_=r2[:, 3, 4 * i:4 * i + 4]), deps=[t_rm4], inc="qy", dma=True))
            rpk2_g[bp] += my_rows_tok

        return un_toks

    def _consume_gseg(b, seg_info, un_toks, i):
        nonlocal prev_scan, last_bc_pair
        bp = b % 2
        if True:
            seg, r8, t_ev, _, _ = seg_info[i]
            r = seg % RING

            def mm_bc(e, bp=bp, i=i):
                ob_ = onesrow_f[32 * bp:32 * bp + 1, 0:48]
                e.matmul(pb[4][0:48, :], ob_,
                         srow[bp][:, 512 * i:512 * (i + 1)], start=True, stop=True)
                return e.matmul(pb[5][0:48, :], ob_,
                                srow[bp][:, T4 + 512 * i:T4 + 512 * (i + 1)],
                                start=True, stop=True)
            t_bc = prog.add("t", mm_bc,
                            deps=[un_toks[2 * i], un_toks[2 * i + 1], tok_or] +
                                 list(last_bc_pair), inc="tmm")
            if i == 3:
                srow_g[bp] = t_bc

            dep_au = [t_bc, t_ev, t_nhbar, t_cc, au_g[r]]

            def ga_a(e, r=r, r8=r8):
                e.tensor_tensor(au[r][0:16, :], vwm[r8][0:16, :], pb[4][0:16, :],
                                ALU.mult)
                return e.scalar_tensor_tensor(
                    au[r][0:16, :], pb[5][0:16, :], nhbar[0:16],
                    au[r][0:16, :], ALU.mult, ALU.add)
            t_a = prog.add("v", ga_a, deps=dep_au, inc="vec")

            def ga_u(e, r=r, r8=r8):
                e.tensor_tensor(au[r][32:48, :], vwm[r8][32:48, :], pb[4][32:48, :],
                                ALU.mult)
                return e.scalar_tensor_tensor(
                    au[r][32:48, :], pb[5][32:48, :], nhbar[32:48],
                    au[r][32:48, :], ALU.mult, ALU.add)
            t_uu = prog.add("v", ga_u, deps=dep_au, inc="vec")
            vwm_g[r8] = t_uu
            t_um = prog.add("v", lambda e, r=r, seg=seg: e.tensor_scalar(
                au[r][32:48, :], au[r][32:48, :], c_col[32:48],
                segfl[32:48, seg:seg + 1], ALU.add, ALU.mult),
                deps=[t_uu, tok_fl], inc="vec")
            t_sg = prog.add("s", lambda e, r=r: e.activation(
                au[r][0:16, :], au[r][0:16, :], AF.Sigmoid,
                bias=c_col[0:16], scale=1.0), deps=[t_a], inc="act")
            last_bc_pair = (t_a, t_um)

            t_sh = prog.add("y", lambda e, r=r, u2=seg % 2: e.dma_start(
                out=ust[u2], in_=au[r][32:48, :]),
                deps=[t_um, ust_g[seg % 2], prev_scan], inc="qy", dma=True)

            u2 = seg % 2

            def scn(e, r=r, u2=u2, seg=seg):
                init = 0.0 if seg == 0 else ust[1 - u2][:, SEG - 1:SEG]
                return e.tensor_tensor_scan(ust[u2], au[r][0:16, :], ust[u2],
                                            init, ALU.mult, ALU.add)
            t_scan = prog.add("v", scn, deps=[t_sh, t_sg, prev_scan], inc="vec")
            prev_scan = t_scan
            au_g[r] = t_scan
            ust_g[seg % 2] = t_scan

            if seg >= NSEG - 4:
                my0 = (seg - (NSEG - 4)) * SEG
                t_cst = prog.add("v", lambda e, u2=seg % 2, my0=my0: e.tensor_copy(
                    stb16[0:16, my0:my0 + SEG], ust[u2]), deps=[t_scan], inc="vec")
                my_state_tok.append(t_cst)
                ust_g[seg % 2] = t_cst
                if DEBUG_OUTS and DEBUG_P1:
                    prog.add("y", lambda e, u2=seg % 2, my0=my0: e.dma_start(
                        out=dbg["st"][:, my0:my0 + SEG], in_=ust[u2]),
                        deps=[t_cst], inc="qy", dma=True)

            if DEBUG_OUTS and DEBUG_P1:
                c0 = seg * SEG
                d1 = prog.add("y", lambda e, r=r, c0=c0: e.dma_start(
                    out=dbg["au"][0:16, c0:c0 + SEG], in_=au[r][0:16, :]),
                    deps=[t_sg, t_um], inc="qy", dma=True)
                d2 = prog.add("y", lambda e, r=r, c0=c0: e.dma_start(
                    out=dbg["au"][16:32, c0:c0 + SEG], in_=au[r][32:48, :]),
                    deps=[t_sg, t_um], inc="qy", dma=True)
                au_g[r] = d2
                prog.add("y", lambda e, u2=seg % 2, c0=c0: e.dma_start(
                    out=dbg["ust"][:, c0:c0 + SEG], in_=ust[u2]),
                    deps=[t_scan], inc="qy", dma=True)
                ust_g[seg % 2] = t_scan

    my_rows_tok = []

    d_xnew = nc.dram_tensor("xnew_sc", [4, 128, 8, 512], F32, kind="Internal").ap()
    d_t23 = nc.dram_tensor("t23_sc", [HC, 128, 1024], BF16, kind="Internal").ap()

    xnt = [A1.at(_off["xb0"] + 8 * 1024, [128, 8, 512], F32),
           A1.at(_off["xb0"] + 24 * 1024, [128, 8, 512], F32)]
    sqf = sq[1]
    m12 = A1.at(_off["xb0"], [33, T], F32)

    pair_guard = [None, None, None, None]
    bcx_guard = None
    xnt_g = [None, None]
    spill_toks = []
    stat_toks = []
    prev_e = [None, None]
    sqf_g = None

    def _p2_tile(tt):
        nonlocal bcx_guard, sqf_g, prev_e
        xv = xnt[tt % 2]
        cols = slice(512 * tt, 512 * (tt + 1))
        dep_x = fold_done + [xnt_g[tt % 2]]
        if tt == 0:
            dep_x += [xb_g[1], xb_g[2]]
        elif tt == 1:
            dep_x += [xb_g[3], sq_g[0]]
        if sqf_g is None:
            sqf_g2 = sq_g[1]
        else:
            sqf_g2 = None
        t_x = prog.add("y", lambda e, xv=xv, tt=tt: e.dma_start(
            out=xv, in_=d_xT[NSEG - 4 + tt]),
            deps=dep_x, inc="qy", dma=True)

        def mm_bcx(e, tt=tt):
            e.matmul(pb[6][0:128, :], onesrow_f[0:1, :],
                     rowT(0)[:, 512 * tt:512 * (tt + 1)], start=True, stop=True)
            return e.matmul(pb[7][0:128, :], onesrow_f[32:33, :].bitcast(F32),
                            rowT(1)[:, 512 * tt:512 * (tt + 1)], start=True, stop=True)
        t_bcx = prog.add("t", mm_bcx,
                         deps=[tok_or, my_state_tok[tt], bcx_guard,
                               prev_e[0], prev_e[1], pm_g[0], pm_g[1]] + my_rows_tok,
                         inc="tmm")
        ep_toks = []
        for dc in range(DC):
            bank = pb[dc % 4]

            def mm_wc2(e, dc=dc, cols=cols, bank=bank):
                return e.matmul(bank[0:128, :], wcT[:, 128 * dc:128 * (dc + 1)],
                                stb16[:, cols], start=True, stop=True)
            depm = [t_wc[0], t_wc[1], tok_bo, tok_st1, pair_guard[dc % 4],
                    pk_g[dc % 4], my_state_tok[tt]]
            t_mmo = prog.add("t", mm_wc2, deps=depm, inc="tmm")

            def ep(e, dc=dc, xv=xv, bank=bank):
                e.tensor_tensor(tmp1, xv[:, dc, :], pb[6][0:128, :], ALU.mult)
                e.scalar_tensor_tensor(tmp2, pb[7][0:128, :], -1.0, tmp1,
                                       ALU.mult, ALU.add)
                return e.tensor_tensor(xv[:, dc, :], tmp2, bank[0:128, :], ALU.add)
            t_ep = prog.add("v", ep, deps=[t_mmo, t_bcx, t_x], inc="vec")
            pair_guard[dc % 4] = t_ep
            ep_toks.append(t_ep)
        bcx_guard = ep_toks[-1]

        t_sqs = []
        for dc in range(DC):
            t_sqs.append(prog.add("s", lambda e, dc=dc, xv=xv: e.activation(
                sqf[:, dc, :], xv[:, dc, :], AF.Square),
                deps=[ep_toks[dc], prev_e[1], sqf_g2], inc="act"))

        def mm_m1(e, xv=xv):
            last = None
            for k in range(DC):
                last = e.matmul(pb[6][0:1, :], onescol_f, xv[:, k, :],
                                start=(k == 0), stop=(k == DC - 1))
            return last
        t_m1 = prog.add("t", mm_m1,
                        deps=ep_toks + [tok_of, prev_e[0]],
                        inc="tmm")

        def mm_m2(e):
            last = None
            for k in range(DC):
                last = e.matmul(pb[7][0:1, :], onescol_b, sqf[:, k, :],
                                start=(k == 0), stop=(k == DC - 1))
            return last
        t_m2 = prog.add("t", mm_m2, deps=t_sqs + [tok_ob], inc="tmm")
        e1 = prog.add("s", lambda e, cols=cols: e.copy(m12[0:1, cols], pb[6][0:1, :]),
                      deps=[t_m1], inc="act")
        e2 = prog.add("s", lambda e, cols=cols: e.copy(m12[32:33, cols], pb[7][0:1, :]),
                      deps=[t_m2], inc="act")
        prev_e = [e1, e2]
        stat_toks.extend([e1, e2])

        t_sp = prog.add("y", lambda e, xv=xv, tt=tt: e.dma_start(
            out=d_xnew[tt], in_=xv), deps=ep_toks, inc="qy", dma=True)
        spill_toks.append(t_sp)
        xnt_g[tt % 2] = t_sp
        if DEBUG_OUTS and DEBUG_P2:
            for dc in range(DC):
                prog.add("y", lambda e, xv=xv, dc=dc, cols=cols: e.dma_start(
                    out=dbg["xnew"][128 * dc:128 * (dc + 1), cols],
                    in_=xv[:, dc, :]), deps=ep_toks, inc="qy", dma=True)

    def _phase1_driver():
        _si = {}
        _un = {}
        for b in range(NB):
            _si[b] = [_produce_seg(b, 0)]
            if b >= 1:
                _un[b - 1] = _consume_rm(b - 1, _si[b - 1])
            _si[b].append(_produce_seg(b, 1))
            if b >= 1:
                _consume_gseg(b - 1, _si[b - 1], _un[b - 1], 0)
            _si[b].append(_produce_seg(b, 2))
            if b >= 1:
                _consume_gseg(b - 1, _si[b - 1], _un[b - 1], 1)
            _si[b].append(_produce_seg(b, 3))
            if b >= 1:
                _consume_gseg(b - 1, _si[b - 1], _un[b - 1], 2)
                _consume_gseg(b - 1, _si[b - 1], _un[b - 1], 3)
                del _si[b - 1], _un[b - 1]
        _un[NB - 1] = _consume_rm(NB - 1, _si[NB - 1])
        for _i in range(4):
            _consume_gseg(NB - 1, _si[NB - 1], _un[NB - 1], _i)
            if _i >= 2:
                _p2_tile(_i - 2)
        _p2_tile(2)
        _p2_tile(3)


    _phase1_driver()

    t_pk1 = prog.add("y", lambda e: e.dma_start(out=rpkf[:, 0, :], in_=m12[0:1, :]),
                     deps=stat_toks, inc="qy", dma=True)
    t_pk2 = prog.add("y", lambda e: e.dma_start(out=rpkf[:, 1, :], in_=m12[32:33, :]),
                     deps=stat_toks, inc="qy", dma=True)

    def frm(e):
        rr = rpkf
        vb(e, e.tensor_scalar(rr[:, 2, :], rr[:, 0, :], INV_D, None, ALU.mult))
        vb(e, e.tensor_scalar(rr[:, 2, :], rr[:, 0, :], INV_D, None, ALU.mult))
        e.tensor_scalar(rr[:, 3, :], rr[:, 1, :], INV_D, None, ALU.mult)
        vb(e, e.tensor_scalar(rr[:, 3, :], rr[:, 1, :], INV_D, None, ALU.mult))
        vb(e, e.tensor_tensor(rr[:, 4, :], rr[:, 2, :], rr[:, 2, :], ALU.mult))
        vb(e, e.tensor_tensor(rr[:, 4, :], rr[:, 3, :], rr[:, 4, :], ALU.subtract))
        vb(e, e.tensor_scalar(rr[:, 4, :], rr[:, 4, :], EPS, None, ALU.add))
        return e.engine_nop()
    t_f1 = prog.add("v", frm, deps=[t_pk1, t_pk2], inc="vec")
    def f2f(e):
        sb(e, e.activation(rpkf[:, 4, :], rpkf[:, 4, :], AF.Sqrt))
        return e.nop()
    t_f2 = prog.add("s", f2f, deps=[t_f1], inc="act")
    def f3f(e):
        vb(e, e.reciprocal(rpkf[:, 5, :], rpkf[:, 4, :]))
        return e.engine_nop()
    t_f3 = prog.add("v", f3f, deps=[t_f2], inc="vec")

    def f4(e):
        vb(e, e.tensor_tensor(rpkf[:, 6, :], rpkf[:, 5, :], rpkf[:, 2, :], ALU.mult))
        return e.engine_nop()
    t_f4 = prog.add("v", f4, deps=[t_f3], inc="vec")
    un_ffn = []
    for i in range(4):
        un_ffn.append(prog.add("y", lambda e, i=i: e.dma_start(
            out=rowT(2)[:, 512 * i:512 * (i + 1)], in_=rpkf[:, 5, 4 * i:4 * i + 4]),
            deps=[t_f3], inc="qy", dma=True))
        un_ffn.append(prog.add("y", lambda e, i=i: e.dma_start(
            out=rowT(3)[:, 512 * i:512 * (i + 1)], in_=rpkf[:, 6, 4 * i:4 * i + 4]),
            deps=[t_f4], inc="qy", dma=True))

    t1b = A0.alloc([128, HC, 1024], BF16)
    x2n = A1.at(_off["srow"], [128, 8, 1024], BF16)
    xnf = xnt
    pan12 = [A1.at(_off["xb0"] + 4096, [128, 8, 256], BF16),
             A1.at(_off["sq0"] + 8 * 1024, [128, 8, 256], BF16)]
    panh = [A1.at(_off["vw"], [128, 32, 256], BF16),
            A1.at(_off["m2b"], [128, 32, 256], BF16)]
    t2st = [A1.at(_off["xb0"] + 2048 * i, [128, 2, 512], BF16) for i in range(2)]

    pan12_use = [t_pk2, stat_toks[-1]]
    panh_use = [my_state_tok[-1], my_state_tok[-1]]
    t1_guard = my_state_tok[-1]
    t2st_g = [t_pk1, t_pk2]
    xnf_g = [spill_toks[2], spill_toks[3]]
    x2n_guard = None
    out_toks = []
    mm_prev = [pair_guard[0], pair_guard[1], pair_guard[2], pair_guard[3]]
    sl_guard = [None, None, None, None]
    bcf_guard = [bcx_guard]

    for j in range(2):
        xh_toks = []
        x2n_toks = []
        for h in range(2):
            tile = 2 * j + h
            t_xh = prog.add("y", lambda e, h=h, tile=tile: e.dma_start(
                out=xnf[h], in_=d_xnew[tile]),
                deps=[spill_toks[tile], xnf_g[h]], inc="qy", dma=True)
            xh_toks.append(t_xh)

            def mm_bcf(e, tile=tile):
                e.matmul(pb[4][0:128, :], onesrow_f[64:65, :],
                         rowT(2)[:, 512 * tile:512 * (tile + 1)], start=True, stop=True)
                return e.matmul(pb[5][0:128, :], onesrow_f[0:1, :],
                                rowT(3)[:, 512 * tile:512 * (tile + 1)],
                                start=True, stop=True)
            t_bcf = prog.add("t", mm_bcf, deps=un_ffn + [tok_or] + bcf_guard,
                             inc="tmm")
            for dc in range(DC):
                def mk_x2n(e, dc=dc, h=h):
                    e.scalar_tensor_tensor(tmp1, pb[5][0:128, :], -1.0,
                                           xnf[h][:, dc, :], ALU.mult, ALU.add)
                    return e.tensor_tensor(x2n[:, dc, 512 * h:512 * (h + 1)],
                                           tmp1, pb[4][0:128, :], ALU.mult)
                dep = [t_bcf, t_xh, x2n_guard]
                if x2n_toks:
                    dep.append(x2n_toks[-1])
                x2n_toks.append(prog.add("v", mk_x2n, deps=dep, inc="vec"))
            bcf_guard = [x2n_toks[-1]]
        x2n_ready = x2n_toks[-1]

        for c in range(HC):
            slot = c % 2

            def ld12(e, c=c, slot=slot):
                return e.dma_start(out=pan12[slot], in_=d_W12[c])
            t_ld = prog.add("g", ld12, deps=[pan12_use[slot]], inc="qg", dma=True)
            sl_toks = [None, None]
            for n in range(2):
                ssl = 2 * (c % 2) + n
                def mmU1(e, slot=slot, n=n):
                    last = None
                    for k in range(DC):
                        last = e.matmul(pb[n][0:128, :], pan12[slot][:, k, 0:128],
                                        x2n[:, k, 512 * n:512 * (n + 1)],
                                        start=(k == 0), stop=(k == DC - 1))
                    return last
                tU1 = prog.add("t", mmU1, deps=[t_ld, x2n_ready, mm_prev[n]],
                               inc="tmm")
                t_sl = prog.add("s", lambda e, c=c, n=n, ssl=ssl: e.activation(
                    silu_st[ssl], pb[n][0:128, :], AF.Silu, bias=0.0,
                    scale=s1c[:, c:c + 1]),
                    deps=[tU1, tok_sv, sl_guard[ssl]], inc="act")
                sl_toks[n] = t_sl
                mm_prev[n] = t_sl
            for n in range(2):
                ssl = 2 * (c % 2) + n
                def mmU2(e, slot=slot, n=n):
                    last = None
                    for k in range(DC):
                        last = e.matmul(pb[2 + n][0:128, :],
                                        pan12[slot][:, k, 128:256],
                                        x2n[:, k, 512 * n:512 * (n + 1)],
                                        start=(k == 0), stop=(k == DC - 1))
                    return last
                tU2 = prog.add("t", mmU2, deps=[t_ld, x2n_ready, mm_prev[2 + n]],
                               inc="tmm")
                dep_t1 = [tU2, sl_toks[n]]
                if c == 0 and n == 0:
                    dep_t1.append(t1_guard)
                tT1 = prog.add("v", lambda e, c=c, n=n, ssl=ssl: e.scalar_tensor_tensor(
                    t1b[:, c, 512 * n:512 * (n + 1)], pb[2 + n][0:128, :],
                    s2c[:, c:c + 1], silu_st[ssl], ALU.mult, ALU.mult),
                    deps=dep_t1, inc="vec")
                mm_prev[2 + n] = tT1
                sl_guard[ssl] = tT1
                pan12_use[slot] = tU2
                last_T1 = tT1

        def h_layer(dsrc, scl_col, kind, rd_guard):
            nonlocal mm_prev
            st_toks_l = []
            npan = 16 if kind == "h" else 4
            for p in range(npan):
                slot = p % 2

                def ldp(e, p=p, slot=slot, dsrc=dsrc):
                    return e.dma_start(out=panh[slot], in_=dsrc[p])
                t_ld = prog.add("g", ldp, deps=[panh_use[slot]], inc="qg", dma=True)
                for q in range(2):
                    oc = 2 * p + q
                    for n in range(2):
                        bk = 2 * q + n

                        def mmH(e, slot=slot, q=q, n=n, bk=bk):
                            last = None
                            for k in range(HC):
                                last = e.matmul(
                                    pb[bk][0:128, :],
                                    panh[slot][:, k, 128 * q:128 * (q + 1)],
                                    t1b[:, k, 512 * n:512 * (n + 1)],
                                    start=(k == 0), stop=(k == HC - 1))
                            return last
                        tH = prog.add("t", mmH,
                                      deps=[t_ld, rd_guard, mm_prev[bk]], inc="tmm")
                        panh_use[slot] = tH
                        if kind == "h":
                            tG = prog.add("s", lambda e, oc=oc, n=n, bk=bk:
                                          e.activation(
                                              t2st[oc % 2][:, n, :], pb[bk][0:128, :],
                                              AF.Gelu, bias=0.0,
                                              scale=scl_col[:, oc:oc + 1]),
                                          deps=[tH, t2st_g[oc % 2]], inc="act")
                            mm_prev[bk] = tG
                            if n == 1:
                                t_st = prog.add("y", lambda e, oc=oc: e.dma_start(
                                    out=d_t23[oc], in_=t2st[oc % 2].rearrange(
                                        "p a b -> p (a b)")),
                                    deps=[mm_prev[2 * q], mm_prev[2 * q + 1]],
                                    inc="qy", dma=True)
                                t2st_g[oc % 2] = t_st
                                st_toks_l.append(t_st)
                        else:
                            dc = oc
                            tO = prog.add("v", lambda e, dc=dc, n=n, bk=bk:
                                          e.scalar_tensor_tensor(
                                              xnf[n][:, dc, :], pb[bk][0:128, :],
                                              scl_col[:, dc:dc + 1], xnf[n][:, dc, :],
                                              ALU.mult, ALU.add),
                                          deps=[tH, tok_sv, xh_toks[n]], inc="vec")
                            mm_prev[bk] = tO
                            st_toks_l.append(tO)
            return st_toks_l

        if DEBUG_OUTS and DEBUG_FFN and j == 0:
            prog.add("y", lambda e: e.dma_start(
                out=dbg["x2n"], in_=x2n.rearrange("p a b -> p (a b)")),
                deps=[x2n_ready], inc="qy", dma=True)
            prog.add("y", lambda e: e.dma_start(
                out=dbg["t1"], in_=t1b.rearrange("p a b -> p (a b)")),
                deps=[last_T1], inc="qy", dma=True)
        h1_st = h_layer(d_H1, s3c, "h", last_T1)
        t_rel = prog.add("y", lambda e: e.dma_start(
            out=t1b.rearrange("p a b -> p (a b)"),
            in_=d_t23.transpose([1, 0, 2])),
            deps=h1_st + [panh_use[0], panh_use[1]], inc="qy", dma=True)
        if DEBUG_OUTS and DEBUG_FFN and j == 0:
            prog.add("y", lambda e: e.dma_start(
                out=dbg["t2"], in_=t1b.rearrange("p a b -> p (a b)")),
                deps=[t_rel], inc="qy", dma=True)
        h2_st = h_layer(d_H2, s4c, "h", t_rel)
        t_rel2 = prog.add("y", lambda e: e.dma_start(
            out=t1b.rearrange("p a b -> p (a b)"),
            in_=d_t23.transpose([1, 0, 2])),
            deps=h2_st + [panh_use[0], panh_use[1]], inc="qy", dma=True)
        if DEBUG_OUTS and DEBUG_FFN and j == 0:
            prog.add("y", lambda e: e.dma_start(
                out=dbg["t3"], in_=t1b.rearrange("p a b -> p (a b)")),
                deps=[t_rel2], inc="qy", dma=True)
        w3_out = h_layer(d_W3, s5c, "w3", t_rel2)

        for n in range(2):
            colbase = 1024 * j + 512 * n
            for dc in range(DC):
                t_st = prog.add("y", lambda e, dc=dc, n=n, colbase=colbase: e.dma_start(
                    out=d_y[128 * dc:128 * (dc + 1), colbase:colbase + 512],
                    in_=xnf[n][:, dc, :]), deps=list(w3_out), inc="qy", dma=True)
                out_toks.append(t_st)
            xnf_g[n] = out_toks[-1]
        x2n_guard = w3_out[-1]
        t1_guard = w3_out[-1]

    if DEBUG_OUTS:
        dbg["rows"] = nc.dram_tensor("dbg_rows", [128, 2048 + 192], F32,
                                     kind="ExternalOutput").ap()
        dbg["stb"] = nc.dram_tensor("dbg_stb", [17, 2048], BF16,
                                    kind="ExternalOutput").ap()
        dbg["xsp"] = nc.dram_tensor("dbg_xsp", [4, 128, 8, 512], F32,
                                    kind="ExternalOutput").ap()
        prog.add("y", lambda e: e.dma_start(out=dbg["stb"], in_=stb16),
                 deps=out_toks, inc="qy", dma=True)
        prog.add("y", lambda e: e.dma_start(out=dbg["xsp"], in_=d_xnew),
                 deps=out_toks, inc="qy", dma=True)
        for _rr, _b in ((0, 0), (1, 32), (2, 64)):
            prog.add("y", lambda e, _rr=_rr, _b=_b: e.dma_start(
                out=dbg["rows"][_rr:_rr + 1, 0:2048], in_=rowsT[_b:_b + 1, :]),
                deps=out_toks, inc="qy", dma=True)
        prog.add("y", lambda e: e.dma_start(
            out=dbg["rows"][:, 2048:2048 + 112],
            in_=rpkf[:, 0:7, :].rearrange("p a b -> p (a b)")),
            deps=out_toks, inc="qy", dma=True)
        dbg["rpk2"] = nc.dram_tensor("dbg_rpk2", [128, 2, 11, 16], F32,
                                     kind="ExternalOutput").ap()
        for _pp in range(2):
            prog.add("y", lambda e, _pp=_pp: e.dma_start(
                out=dbg["rpk2"][:, _pp], in_=rpk2[_pp][:, 0:11, :]),
                deps=out_toks, inc="qy", dma=True)
        prog.add("y", lambda e: e.dma_start(
            out=dbg["rows"][0:1, 2048 + 128:2048 + 160].bitcast(F32) if False
            else dbg["rows"][:, 2048 + 160:2048 + 176], in_=rows4),
            deps=out_toks, inc="qy", dma=True)

    prog.add("y", lambda e: e.nop(), deps=out_toks)

    with nc.Block() as block:
        prog.emit(block)
    return nc


_CACHE = {}


def _prep_shared(inputs):
    f = np.float32

    def c(a):
        return np.ascontiguousarray(np.asarray(a, dtype=f))
    x = c(inputs["L0_attn_x"]).reshape(S, D)
    xT = np.ascontiguousarray(x.T)
    Wi = c(inputs["L0_attn_W_in"])
    AB = np.zeros((E, NG), dtype=f)
    AB[:, 0:16] = c(inputs["L0_attn_A"])
    AB[:, 32:48] = c(inputs["L0_attn_B_ssm"])
    Cs = c(inputs["L0_attn_C_ssm"])
    WoT = np.ascontiguousarray(c(inputs["L0_attn_W_out"]).T)
    w1T = c(inputs["L0_ffn_w1_w"]).T
    w2T = c(inputs["L0_ffn_w2_w"]).T
    W12 = np.empty((D, 2 * H_), dtype=f)
    for i in range(HC):
        W12[:, 256 * i:256 * i + 128] = w1T[:, 128 * i:128 * (i + 1)]
        W12[:, 256 * i + 128:256 * i + 256] = w2T[:, 128 * i:128 * (i + 1)]

    def _paneled(wT, npan):
        K, O = wT.shape
        kc = K // 128
        opan = O // npan
        return np.ascontiguousarray(
            wT.reshape(kc, 128, npan, opan).transpose(2, 1, 0, 3))
    sv = np.zeros((128, 144), dtype=f)
    sv[:, 0:32] = c(inputs["L0_ffn_w1_s"]).reshape(32, 128).T
    sv[:, 32:64] = c(inputs["L0_ffn_w2_s"]).reshape(32, 128).T
    sv[:, 64:96] = c(inputs["L0_ffn_h1_s"]).reshape(32, 128).T
    sv[:, 96:128] = c(inputs["L0_ffn_h2_s"]).reshape(32, 128).T
    sv[:, 128:136] = c(inputs["L0_ffn_w3_s"]).reshape(8, 128).T
    sv[:, 136:144] = c(inputs["L0_attn_norm_w"]).reshape(8, 128).T
    bv = np.zeros((128, 24), dtype=f)
    bv[:, 0:16] = c(inputs["L0_attn_b_in"]).reshape(16, 128).T
    bv[:, 16:24] = c(inputs["L0_attn_norm_b"]).reshape(8, 128).T
    shared = {"Wi": Wi, "AB": AB, "Cs": Cs, "WoT": WoT,
              "W12T": _paneled(W12, HC),
              "H1T": _paneled(c(inputs["L0_ffn_h1_w"]).T, 16),
              "H2T": _paneled(c(inputs["L0_ffn_h2_w"]).T, 16),
              "W3T": _paneled(c(inputs["L0_ffn_w3_w"]).T, 4),
              "svecs": sv, "bvecs": bv,
              "bout": c(inputs["L0_attn_b_out"]).reshape(1, D)}
    return xT, shared


def make_in_maps(inputs):
    xT, shared = _prep_shared(inputs)
    in_maps = []
    for cidx in range(NCORE):
        pad = T * (NCORE - 1 - cidx)
        xTc = np.roll(xT, pad, axis=1) if pad else xT
        xt4 = np.ascontiguousarray(
            xTc.reshape(DC, 128, NSEG, SEG).transpose(2, 1, 0, 3))
        fl = np.zeros((48, NSEG), dtype=np.float32)
        fl[:, (pad // SEG):] = 1.0
        m = dict(shared)
        m["xT"] = xt4
        m["segfl"] = fl
        in_maps.append(m)
    return in_maps


def kernel(**inputs):
    if "nc" not in _CACHE:
        _CACHE["nc"] = build_nc()
    nc = _CACHE["nc"]
    in_maps = make_in_maps(inputs)
    res = run_bass_kernel_spmd(nc, in_maps, core_ids=list(range(NCORE)))
    y = np.concatenate([np.asarray(r["y"]) for r in res.results], axis=1)
    return np.ascontiguousarray(y.T).reshape(1, S, D)
